# version 07
# baseline (speedup 1.0000x reference)
import contextlib
import numpy as np

import concourse.bass as bass
import concourse.mybir as mybir
from concourse.bass_utils import run_bass_kernel_spmd

F32 = mybir.dt.float32
BF16 = mybir.dt.bfloat16
AF = mybir.ActivationFunctionType
ALU = mybir.AluOpType
AX = mybir.AxisListType

MAGIC = 12582912.0
GNUDGE = 0.0
INV_SQRT_HD = 0.08838834764831845
S = 2048
HT = 1024
D = 2048
H = 8192
NHC = 8
PAIRS = [[0, 1], [2, 3], [4, 5], [6, 7]]
ALL8 = [list(range(8))]

DBG = set()
STAGE = 99
NHEADS = 8
SUBQ = True
SUBF = True
QINTER = True


class Clock:

    def __init__(self, nc, ctx):
        self.nc = nc
        self.sem = {}
        self.cnt = {}
        self.waited = {}
        for name in ("pe", "dve", "act", "pool", "sp", "dsp", "dact", "dpool", "cc"):
            self.sem[name] = ctx.enter_context(nc.semaphore("sem_" + name))
            self.cnt[name] = 0

    def mark(self, inst, prod, n=1):
        self.cnt[prod] += n
        inst.then_inc(self.sem[prod], n)
        return self.cnt[prod]

    def tick(self, prod, n=1):
        self.cnt[prod] += n
        return self.cnt[prod]

    def wait(self, eng, consumer, prod, val=None):
        val = self.cnt[prod] if val is None else val
        w = self.waited.setdefault(consumer, {})
        if w.get(prod, -1) >= val:
            return
        eng.wait_ge(self.sem[prod], val)
        w[prod] = val


class KB:

    ENG = {"pe": "tensor", "dve": "vector", "act": "scalar", "pool": "gpsimd", "sp": "sync"}
    DQ = {"sp": "dsp", "act": "dact", "pool": "dpool"}

    def __init__(self):
        nc = bass.Bass(trn_type="TRN2", target_bir_lowering=False)
        self.nc = nc
        self.ctx = contextlib.ExitStack()
        self.bufs = {}
        self.dbg_outs = {}

        dt = lambda n, sh, k: nc.dram_tensor(n, sh, F32, kind=k).ap()
        self.xseq = dt("xseq", [S, D], "ExternalInput")
        self.xhalf = dt("xhalf", [HT, D], "ExternalInput")
        self.wqkv = dt("wqkv", [D, 3 * NHC * 128], "ExternalInput")
        self.bqkv = dt("bqkv", [1, 3 * NHC * 128], "ExternalInput")
        self.wout = dt("wout", [NHC * 128, D], "ExternalInput")
        self.bout = dt("bout", [1, D], "ExternalInput")
        self.fob = dt("fob", [1, D], "ExternalInput")
        self.gbi = dt("gbi", [1, H], "ExternalInput")
        self.vbi = dt("vbi", [1, H], "ExternalInput")
        self.gsh = dt("gsh", [H, 256], "ExternalInput")
        self.vsh = dt("vsh", [H, 256], "ExternalInput")
        self.osh = dt("osh", [D, 1024], "ExternalInput")
        self.y = dt("y", [HT, D], "ExternalOutput")
        self.gam_in = nc.dram_tensor("gam_in", [1, 4], F32).ap()
        self.gam_out = nc.dram_tensor("gam_out", [1, 4], F32).ap()
        self.agg_in = nc.dram_tensor("agg_in", [256, H], BF16).ap()
        self.agg_out = nc.dram_tensor("agg_out", [D, H], BF16).ap()
        self.agv_in = nc.dram_tensor("agv_in", [256, H], BF16).ap()
        self.agv_out = nc.dram_tensor("agv_out", [D, H], BF16).ap()
        self.ago_in = nc.dram_tensor("ago_in", [1024, D], BF16).ap()
        self.ago_out = nc.dram_tensor("ago_out", [H, D], BF16).ap()
        self.rs_in = nc.dram_tensor("rs_in", [2 * D, HT], F32).ap()
        self.rs_out = nc.dram_tensor("rs_out", [D, HT], F32).ap()
        self.zdr = nc.dram_tensor("zdr", [HT, H], F32).ap()

    def sb(self, name, shape, dtype):
        cm = self.nc.sbuf_tensor(name, shape, dtype)
        t = cm.__enter__()
        self.bufs[name] = (cm, t)
        return t

    def ps(self, name, shape, dtype=F32):
        cm = self.nc.psum_tensor(name, shape, dtype)
        t = cm.__enter__()
        self.bufs[name] = (cm, t)
        return t

    def free(self, *names):
        for name in names:
            cm, _ = self.bufs.pop(name)
            cm.__exit__(None, None, None)

    def on(self, eng):
        def deco(f):
            getattr(self.blk, self.ENG[eng])(f)
        return deco

    def dbg(self, name, shape, dtype=F32):
        if name not in DBG:
            return None
        if name not in self.dbg_outs:
            self.dbg_outs[name] = self.nc.dram_tensor("dbg_" + name, shape, dtype,
                                                      kind="ExternalOutput").ap()
        return self.dbg_outs[name]

    def build(self):
        nc = self.nc
        with self.ctx:
            self.clk = Clock(nc, self.ctx)
            self.blk = self.ctx.enter_context(nc.Block())
            self.p_consts()
            self.p_ln()
            self.p_gamma_partials()
            self.p_heads()
            self.p_shard_quant()
            self.barrier("bh")
            self.p_woutpartial()
            self.barrier("bw")
            self.p_qT()
            self.barrier("bq")
            self.p_gateval()
            self.barrier("bg")
            self.p_qzT()
            self.barrier("bz")
            self.p_outproj()
            self.p_finish()
        return nc

    def p_consts(self):
        clk = self.clk
        ident_b = self.sb("ident_b", [128, 128], BF16)
        ident_f = self.sb("ident_f", [128, 128], F32)
        diag = self.sb("diag", [128, 128], BF16)
        ones1_f = self.sb("ones1_f", [1, 128], F32)
        ones1_b = self.sb("ones1_b", [1, 128], BF16)
        onescol_f = self.sb("onescol_f", [128, 1], F32)
        onescol_b = self.sb("onescol_b", [128, 1], BF16)
        eps1 = self.sb("eps1", [128, 1], F32)
        bq_cols = self.sb("bq_cols", [128, 24], F32)
        bofo_col = self.sb("bofo_col", [128, 16], F32)
        gamcols = self.sb("gamcols", [128, 8], F32)
        gxcol = self.sb("gxcol", [128, 8], F32)
        gzcol = self.sb("gzcol", [128, 8], F32)
        zrow_b = self.sb("zrow_b", [1, 512], BF16)
        self.settle_col = self.sb("settle_col", [1, 8], F32)
        bq_row = self.sb("bq_row", [1, 3072], F32)
        bo_row = self.sb("bo_row", [1, D], F32)
        fo_row = self.sb("fo_row", [1, D], F32)
        pmisc = self.ps("pmisc", [128, 512], F32)

        @self.on("sp")
        def _(sp):
            clk.mark(sp.dma_start(out=bq_row[:], in_=self.bqkv), "dsp", 16)
            clk.mark(sp.dma_start(out=bo_row[:], in_=self.bout), "dsp", 16)
            clk.mark(sp.dma_start(out=fo_row[:], in_=self.fob), "dsp", 16)
        self.dma_done_consts = clk.cnt["dsp"]

        @self.on("pool")
        def _(g):
            g.memset(ident_b[:], 0.0)
            g.affine_select(out=ident_b[:], in_=ident_b[:], compare_op=ALU.not_equal,
                            fill=1.0, base=0, pattern=[[-1, 128]], channel_multiplier=1)
            g.memset(ident_f[:], 0.0)
            g.affine_select(out=ident_f[:], in_=ident_f[:], compare_op=ALU.not_equal,
                            fill=1.0, base=0, pattern=[[-1, 128]], channel_multiplier=1)
            g.memset(diag[:], 1.0)
            g.affine_select(out=diag[:], in_=diag[:], compare_op=ALU.is_ge,
                            fill=0.0, base=0, pattern=[[1, 128]], channel_multiplier=-1)
            g.memset(ones1_f[:], 1.0)
            g.memset(ones1_b[:], 1.0)
            g.memset(onescol_f[:], 1.0)
            g.memset(onescol_b[:], 1.0)
            g.memset(zrow_b[:], 0.0)
            clk.mark(g.memset(eps1[:], 1e-5), "pool", 1)
        self.consts_ready = clk.cnt["pool"]

        @self.on("pe")
        def _(te):
            clk.wait(te, "pe", "dsp", self.dma_done_consts)
            clk.wait(te, "pe", "pool", self.consts_ready)
            one11 = ones1_f[0:1, 0:1]
            last = None
            for e in range(24):
                last = te.matmul(pmisc[:, e:e + 1], bq_row[0:1, 128 * e:128 * (e + 1)], one11, start=True, stop=True)
            for e in range(16):
                last = te.matmul(pmisc[:, 24 + e:25 + e], bo_row[0:1, 128 * e:128 * (e + 1)], one11, start=True, stop=True)
            for e in range(16):
                last = te.matmul(pmisc[:, 40 + e:41 + e], fo_row[0:1, 128 * e:128 * (e + 1)], one11, start=True, stop=True)
            clk.mark(last, "pe", 1)

        @self.on("dve")
        def _(ve):
            clk.wait(ve, "dve", "pe", clk.cnt["pe"])
            ve.tensor_copy(bq_cols[:], pmisc[:, 0:24])
            clk.mark(ve.tensor_copy(bofo_col[:], pmisc[:, 24:40]), "dve", 1)
            ve.wait_ge(clk.sem["dve"], clk.cnt["dve"])
            clk.mark(ve.tensor_add(bofo_col[:], bofo_col[:], pmisc[:, 40:56]), "dve", 1)
        self.bq_cols_ready = clk.cnt["dve"]
        self.bofo_ready = clk.cnt["dve"]
        self.free("pmisc", "fo_row", "bo_row", "bq_row")

    def ln_tile(self, ve, se, xt, sq, mneg, rstd, scr1):
        clk = self.clk
        ve.tensor_reduce(scr1[:, 0:1], xt[:], AX.X, ALU.add)
        ve.tensor_mul(sq[:], xt[:], xt[:])
        clk.mark(ve.tensor_reduce(scr1[:, 1:2], sq[:], AX.X, ALU.add), "dve", 1)
        ve.wait_ge(clk.sem["dve"], clk.cnt["dve"])
        ve.tensor_scalar(mneg[:], scr1[:, 0:1], -1.0 / D, None, ALU.mult)
        clk.mark(ve.tensor_scalar(scr1[:, 2:3], scr1[:, 1:2], 1.0 / D, None, ALU.mult), "dve", 1)
        ve.wait_ge(clk.sem["dve"], clk.cnt["dve"])
        clk.mark(ve.tensor_mul(scr1[:, 3:4], mneg[:], mneg[:]), "dve", 1)
        ve.wait_ge(clk.sem["dve"], clk.cnt["dve"])
        clk.mark(ve.tensor_sub(scr1[:, 4:5], scr1[:, 2:3], scr1[:, 3:4]), "dve", 1)

    def p_ln(self):
        clk = self.clk
        self.OT = [self.sb(f"OT{j}", [128, S], BF16) for j in range(8)]
        self.xnT = [self.sb(f"xnT{j}", [128, S], BF16) for j in range(16)]
        sq = self.sb("sq", [128, D], F32)
        xt = [self.sb(f"xt{i}", [128, D], F32) for i in range(2)]
        xnb = [self.sb(f"xnb{i}", [128, D], BF16) for i in range(2)]
        scr1 = self.sb("scr1", [128, 8], F32)
        mneg = self.sb("mneg", [128, 1], F32)
        rstd = self.sb("rstd", [128, 1], F32)
        ptr = [self.ps(f"ptr{i}", [128, 2048], BF16) for i in range(2)]

        xtiles = self.xseq.rearrange("(a p) f -> a p f", p=128)
        dbg_xnT = self.dbg("xnT", [D, S], BF16)
        self.ln_cast_done = {}
        self.ln_evac_done = {}
        self.ln_tr_mark = {}

        for i in range(16):
            bi = i % 2

            @self.on("sp")
            def _(sp, i=i, bi=bi):
                if i >= 2:
                    clk.wait(sp, "sp", "dve", self.ln_cast_done[i - 2])
                clk.mark(sp.dma_start(out=xt[bi][:], in_=xtiles[i]), "dsp", 16)
            dma_x = clk.cnt["dsp"]

            @self.on("dve")
            def _(ve, i=i, bi=bi, dma_x=dma_x):
                clk.wait(ve, "dve", "dsp", dma_x)
                if i == 0:
                    clk.wait(ve, "dve", "pool", self.consts_ready)
                self.ln_tile(ve, None, xt[bi], sq, mneg, rstd, scr1)

            @self.on("act")
            def _(se, i=i):
                clk.wait(se, "act", "dve", clk.cnt["dve"])
                clk.mark(se.activation(rstd[:], scr1[:, 4:5], AF.Sqrt, bias=self.bufs["eps1"][1][:]), "act", 1)

            @self.on("dve")
            def _(ve, i=i, bi=bi):
                clk.wait(ve, "dve", "act", clk.cnt["act"])
                clk.mark(ve.reciprocal(rstd[:], rstd[:]), "dve", 1)
                ve.wait_ge(clk.sem["dve"], clk.cnt["dve"])
                if i >= 2:
                    clk.wait(ve, "dve", "pe", self.ln_tr_mark[i - 2])
                clk.mark(ve.tensor_scalar(xnb[bi][:], xt[bi][:], mneg[:], rstd[:], ALU.add, ALU.mult), "dve", 1)
            self.ln_cast_done[i] = clk.cnt["dve"]

            @self.on("pe")
            def _(te, i=i, bi=bi):
                clk.wait(te, "pe", "dve", self.ln_cast_done[i])
                if i >= 2:
                    clk.wait(te, "pe", "dve", self.ln_evac_done[i - 2])
                last = None
                for j in range(16):
                    last = te.transpose(ptr[bi][:, 128 * j:128 * (j + 1)],
                                        xnb[bi][:, 128 * j:128 * (j + 1)],
                                        self.bufs["ident_b"][1][:])
                clk.mark(last, "pe", 1)
                self.ln_tr_mark[i] = clk.cnt["pe"]

            @self.on("dve")
            def _(ve, i=i, bi=bi):
                clk.wait(ve, "dve", "pe", clk.cnt["pe"])
                last = None
                for j in range(16):
                    last = ve.tensor_copy(self.xnT[j][:, 128 * i:128 * (i + 1)], ptr[bi][:, 128 * j:128 * (j + 1)])
                clk.mark(last, "dve", 1)
            self.ln_evac_done[i] = clk.cnt["dve"]
        self.xnT_done = clk.cnt["dve"]

        if dbg_xnT is not None:
            @self.on("act")
            def _(se):
                clk.wait(se, "act", "dve", self.xnT_done)
                for j in range(16):
                    clk.mark(se.dma_start(out=dbg_xnT[128 * j:128 * (j + 1), :], in_=self.xnT[j][:]), "dact", 16)

        self.free("ptr1", "ptr0")
        self.free("rstd", "mneg", "scr1", "xnb1", "xnb0", "xt1", "xt0", "sq")

    def settle(self, ve):
        return ve.tensor_copy(self.settle_col[0:1, 4:8], self.settle_col[0:1, 0:4])

    def fence(self, te, psum_ap, n):
        return te.matmul(psum_ap, self.bufs["ones1_b"][1][:],
                         self.bufs["zrow_b"][1][0:1, 0:n], start=False, stop=True)

    def barrier(self, tag):
        clk = self.clk
        prods = ("pe", "dve", "act", "pool", "dsp", "dact", "dpool", "cc")
        snap = {p: clk.cnt[p] for p in prods}
        for eng in ("pe", "dve", "act", "pool", "sp"):
            @self.on(eng)
            def _(e, eng=eng):
                for p in prods:
                    if p != eng:
                        clk.wait(e, eng, p, snap[p])

    def transpose_batch(self, tag, jobs, src_wait, extra_pe_waits=()):
        clk = self.clk
        pts = [self.ps(f"ptb_{tag}_{i}", [128, 2048], BF16) for i in range(2)]
        evac_mark = {}
        batches = [jobs[i:i + 16] for i in range(0, len(jobs), 16)]
        for bi, batch in enumerate(batches):
            @self.on("pe")
            def _(te, bi=bi, batch=batch):
                clk.wait(te, "pe", *src_wait)
                for pw in extra_pe_waits:
                    clk.wait(te, "pe", *pw)
                if bi >= 2:
                    clk.wait(te, "pe", "dve", evac_mark[bi - 2])
                last = None
                for k, (src, dst) in enumerate(batch):
                    last = te.transpose(pts[bi % 2][:, 128 * k:128 * (k + 1)], src, self.bufs["ident_b"][1][:])
                clk.mark(last, "pe", 1)
            pe_mark = clk.cnt["pe"]

            @self.on("dve")
            def _(ve, bi=bi, batch=batch, pe_mark=pe_mark):
                clk.wait(ve, "dve", "pe", pe_mark)
                last = None
                for k, (src, dst) in enumerate(batch):
                    last = ve.tensor_copy(dst, pts[bi % 2][:, 128 * k:128 * (k + 1)])
                clk.mark(last, "dve", 1)
            evac_mark[bi] = clk.cnt["dve"]
        self.free(f"ptb_{tag}_1", f"ptb_{tag}_0")
        return clk.cnt["dve"]

    def p_gamma_partials(self):
        if STAGE < 4:
            return
        clk = self.clk
        self.gacc = self.sb("gacc", [128, 28], F32)
        self.grow = self.sb("grow", [1, 12], F32)
        self.sht = [self.sb(f"sht{i}", [128, 2048], F32) for i in range(2)]
        self.tern = self.sb("tern", [128, 2048], BF16)
        self.stg = [self.sb(f"stg{i}", [128, 2048], BF16) for i in range(8)]
        self.shards = [("g", self.gsh, self.agg_in), ("v", self.vsh, self.agv_in), ("o", self.osh, self.ago_in)]
        self.sh_dma = {}
        self.gam_ready = None
        self.ag_mark = {}

    def emit_gamma_slice(self, sh, sl):
        clk = self.clk
        name, ap, _ = self.shards[sh]
        src = ap.rearrange("(b t p) f -> b p t f", t=(8 if sh < 2 else 2), p=128)
        nb = (sh * 8 + sl) % 2

        @self.on("sp")
        def _(sp):
            if sh * 8 + sl < 2:
                clk.wait(sp, "sp", "dve", self.xnT_done)
            else:
                clk.wait(sp, "sp", "dve", self.gam_slice_done[sh * 8 + sl - 2])
            clk.mark(sp.dma_start(out=self.sht[nb][:], in_=src[sl]), "dsp", 16)
        dmam = clk.cnt["dsp"]

        @self.on("dve")
        def _(ve):
            clk.wait(ve, "dve", "dsp", dmam)
            clk.mark(ve.tensor_reduce(self.gacc[:, 8 * sh + sl:8 * sh + sl + 1], self.sht[nb][:],
                                      AX.X, ALU.add, apply_absolute_value=True), "dve", 1)
        if not hasattr(self, "gam_slice_done"):
            self.gam_slice_done = {}
        self.gam_slice_done[sh * 8 + sl] = clk.cnt["dve"]

    def emit_gamma_finishA(self):
        clk = self.clk
        onescol_f = self.bufs["onescol_f"][1]
        pgm = self.ps("pgm", [128, 512], F32)

        @self.on("dve")
        def _(ve):
            ve.wait_ge(clk.sem["dve"], self.gam_slice_done[23])
            ve.tensor_reduce(self.gacc[:, 24:25], self.gacc[:, 0:8], AX.X, ALU.add)
            ve.tensor_reduce(self.gacc[:, 25:26], self.gacc[:, 8:16], AX.X, ALU.add)
            clk.mark(ve.tensor_reduce(self.gacc[:, 26:27], self.gacc[:, 16:24], AX.X, ALU.add), "dve", 1)
        red_mark = clk.cnt["dve"]

        @self.on("pe")
        def _(te):
            clk.wait(te, "pe", "dve", red_mark)
            clk.mark(te.matmul(pgm[0:1, 0:4], onescol_f[:], self.gacc[:, 24:28], start=True, stop=True), "pe", 1)

        @self.on("dve")
        def _(ve):
            clk.wait(ve, "dve", "pe", clk.cnt["pe"])
            clk.mark(ve.tensor_copy(self.grow[0:1, 0:4], pgm[0:1, 0:4]), "dve", 1)
        self.pe_gate_dve = clk.cnt["dve"]

        @self.on("act")
        def _(se):
            clk.wait(se, "act", "dve", clk.cnt["dve"])
            clk.mark(se.dma_start(out=self.gam_in, in_=self.grow[0:1, 0:4]), "dact", 16)
        gin_mark = clk.cnt["dact"]

        @self.on("pool")
        def _(g):
            clk.wait(g, "pool", "dact", gin_mark)
            clk.mark(g.collective_compute("AllReduce", ALU.add, replica_groups=ALL8,
                                          ins=[self.gam_in.opt()], outs=[self.gam_out.opt()]), "cc", 1)
        self.ar_mark = clk.cnt["cc"]
        self.free("pgm")

    def emit_gamma_finishB(self):
        clk = self.clk
        ones1_f = self.bufs["ones1_f"][1]
        gamcols = self.bufs["gamcols"][1]
        pgm2 = self.ps("pgm2", [128, 512], F32)

        @self.on("sp")
        def _(sp):
            clk.wait(sp, "sp", "cc", self.ar_mark)
            clk.mark(sp.dma_start(out=self.grow[0:1, 0:4], in_=self.gam_out), "dsp", 16)
        gout_mark = clk.cnt["dsp"]

        @self.on("dve")
        def _(ve):
            clk.wait(ve, "dve", "dsp", gout_mark)
            clk.mark(ve.tensor_scalar(self.grow[0:1, 0:4], self.grow[0:1, 0:4],
                                      1.0 / (H * D), 1e-5, ALU.mult, ALU.max), "dve", 1)
            ve.wait_ge(clk.sem["dve"], clk.cnt["dve"])
            if GNUDGE:
                clk.mark(ve.tensor_scalar(self.grow[0:1, 4:8], self.grow[0:1, 0:4],
                                          GNUDGE, None, ALU.mult), "dve", 1)
                ve.wait_ge(clk.sem["dve"], clk.cnt["dve"])
                clk.mark(ve.tensor_add(self.grow[0:1, 0:4], self.grow[0:1, 0:4],
                                       self.grow[0:1, 4:8]), "dve", 1)
                ve.wait_ge(clk.sem["dve"], clk.cnt["dve"])
            ve.tensor_scalar(self.grow[0:1, 4:8], self.grow[0:1, 0:4], 1.0 / 127.0, None, ALU.mult)
            clk.mark(ve.reciprocal(self.grow[0:1, 8:12], self.grow[0:1, 0:4]), "dve", 1)
        gr_mark = clk.cnt["dve"]

        @self.on("pe")
        def _(te):
            clk.wait(te, "pe", "dve", gr_mark)
            clk.wait(te, "pe", "dve", self.pe_gate_dve)
            clk.mark(te.matmul(pgm2[:, 0:8], ones1_f[:], self.grow[0:1, 4:12], start=True, stop=True), "pe", 1)

        @self.on("dve")
        def _(ve):
            clk.wait(ve, "dve", "pe", clk.cnt["pe"])
            clk.mark(ve.tensor_copy(gamcols[:], pgm2[:, 0:8]), "dve", 1)
        self.gam_ready = clk.cnt["dve"]
        self.pe_gate_dve = clk.cnt["dve"]
        self.free("pgm2")
        dbg_gam = self.dbg("gam", [1, 4])
        if dbg_gam is not None:
            @self.on("act")
            def _(se):
                clk.wait(se, "act", "dve", self.gam_ready)
                clk.mark(se.dma_start(out=dbg_gam, in_=self.grow[0:1, 0:4]), "dact", 16)

    def emit_quant_slice(self, sh, sl):
        clk = self.clk
        name, ap, _ = self.shards[sh]
        gamcols = self.bufs["gamcols"][1]
        tpr = 8 if sh < 2 else 2
        src = ap.rearrange("(b t p) f -> b p t f", t=tpr, p=128)
        nb = sl % 2
        if not hasattr(self, "q_used"):
            self.q_used = {}
            self.q_tern_free = 0

        @self.on("sp")
        def _(sp):
            clk.wait(sp, "sp", "dve", self.gam_slice_done[23])
            prev = self.q_used.get((sh, sl - 1)) or self.q_used.get((sh - 1, 7))
            if prev:
                clk.wait(sp, "sp", "dve", prev)
            clk.mark(sp.dma_start(out=self.sht[nb][:], in_=src[sl]), "dsp", 16)
        dmam = clk.cnt["dsp"]

        @self.on("dve")
        def _(ve):
            clk.wait(ve, "dve", "dsp", dmam)
            ve.wait_ge(clk.sem["dve"], self.gam_ready)
            clk.wait(ve, "dve", "pe", self.q_tern_free)
            clk.mark(ve.tensor_scalar(self.sht[nb][:], self.sht[nb][:], gamcols[:, 4 + sh:5 + sh],
                                      MAGIC, ALU.mult, ALU.add), "dve", 1)
            ve.wait_ge(clk.sem["dve"], clk.cnt["dve"])
            clk.mark(ve.tensor_scalar(self.sht[nb][:], self.sht[nb][:], MAGIC - 1.0, MAGIC + 1.0,
                                      ALU.max, ALU.min), "dve", 1)
            ve.wait_ge(clk.sem["dve"], clk.cnt["dve"])
            clk.mark(ve.tensor_scalar(self.tern[:], self.sht[nb][:], -MAGIC, None, ALU.add), "dve", 1)
        self.q_used[(sh, sl)] = clk.cnt["dve"]
        tern_mark = clk.cnt["dve"]

        jobs = []
        if sh < 2:
            for t in range(8):
                for c in range(2):
                    srcap = self.tern[:, 256 * t + 128 * c:256 * t + 128 * (c + 1)]
                    col = 1024 * sl + 128 * t
                    dst = self.stg[4 * c + col // 2048][:, col % 2048:col % 2048 + 128]
                    jobs.append((srcap, dst))
        else:
            for t in range(2):
                for c in range(8):
                    srcap = self.tern[:, 1024 * t + 128 * c:1024 * t + 128 * (c + 1)]
                    dst = self.stg[c][:, 256 * sl + 128 * t:256 * sl + 128 * (t + 1)]
                    jobs.append((srcap, dst))
        tm = self.transpose_batch(f"qs{sh}_{sl}", jobs, ("dve", tern_mark),
                                  extra_pe_waits=((("dact", self.stg_free.get(sh, 0)),)
                                                  if hasattr(self, "stg_free") else ()))
        self.q_tern_free = clk.cnt["pe"]
        self.pe_gate_dve = max(getattr(self, "pe_gate_dve", 0), tm)
        return tm

    def emit_shard_out(self, sh):
        clk = self.clk
        name, _, ag_in = self.shards[sh]
        ag_out = {"g": self.agg_out, "v": self.agv_out, "o": self.ago_out}[name]
        @self.on("dve")
        def _(ve):
            ve.wait_ge(clk.sem["dve"], clk.cnt["dve"])
            clk.mark(self.settle(ve), "dve", 1)
        qm = self.clk.cnt["dve"]

        @self.on("act")
        def _(se):
            clk.wait(se, "act", "dve", qm)
            if sh < 2:
                for c in range(2):
                    for s4 in range(4):
                        clk.mark(se.dma_start(out=ag_in[128 * c:128 * (c + 1), 2048 * s4:2048 * (s4 + 1)],
                                              in_=self.stg[4 * c + s4][:]), "dact", 16)
            else:
                for c in range(8):
                    clk.mark(se.dma_start(out=ag_in[128 * c:128 * (c + 1), :], in_=self.stg[c][:]), "dact", 16)
        stgd = clk.cnt["dact"]

        @self.on("pool")
        def _(g):
            clk.wait(g, "pool", "dact", stgd)
            clk.mark(g.collective_compute("AllGather", ALU.bypass, replica_groups=ALL8,
                                          ins=[ag_in.opt()], outs=[ag_out.opt()]), "cc", 1)
        if not hasattr(self, "stg_free"):
            self.stg_free = {}
        self.stg_free[sh + 1] = clk.cnt["dact"]
        self.ag_mark[name] = clk.cnt["cc"]

        dbg = self.dbg("tern" + name, [256 if sh < 2 else 1024, H if sh < 2 else D], BF16)
        if dbg is not None:
            @self.on("act")
            def _(se):
                clk.wait(se, "act", "cc", clk.cnt["cc"])
                clk.mark(se.dma_start(out=dbg, in_=ag_in), "dact", 16)

    def p_heads(self):
        clk = self.clk
        ident_b = self.bufs["ident_b"][1]
        diag = self.bufs["diag"][1]
        ones1_f = self.bufs["ones1_f"][1]
        onescol_b = self.bufs["onescol_b"][1]
        bq_cols = self.bufs["bq_cols"][1]

        whb = [self.sb(f"whb{i}", [128, 16 * 384], BF16) for i in range(2)]
        QTh = self.sb("QTh", [128, S], BF16)
        KTh = self.sb("KTh", [128, S], BF16)
        VTh = self.sb("VTh", [128, S], BF16)
        vtm = self.sb("vtm", [128, S], BF16)
        sqb = self.sb("sqb", [128, S], BF16)
        Pb = [self.sb(f"Pb{i}", [128, 1024], BF16) for i in range(2)]
        binv = self.sb("binv", [128, 1024], F32)
        linv = self.sb("linv", [1, 1024], F32)
        scrC = self.sb("scrC", [1, 16], F32)
        nC = self.sb("nC", [128, 1], F32)

        dbg_QT = self.dbg("QT", [1024, S], BF16)
        dbg_KT = self.dbg("KT", [1024, S], BF16)
        dbg_OT = self.dbg("OT", [1024, S], BF16)

        wtiles = self.wqkv.rearrange("(a p) f -> a p f", p=128)
        whb_dma = [0, 0]
        evac_done = {}
        prev_ot_mark = 0

        nheads = NHEADS if STAGE >= 1 else 0
        for h in range(nheads):
            hb = h % 2

            @self.on("pool")
            def _(g, h=h, hb=hb):
                if h < 2:
                    clk.wait(g, "pool", "dve", self.xnT_done)
                if h >= 2:
                    clk.wait(g, "pool", "dve", evac_done[h - 2])
                last = None
                for j in range(16):
                    last = g.dma_start(out=whb[hb][:, 384 * j:384 * (j + 1)],
                                       in_=wtiles[j][:, 384 * h:384 * (h + 1)])
                    clk.mark(last, "dpool", 16)
            whb_dma[hb] = clk.cnt["dpool"]

            pq = [self.ps(f"pq{h}_{i}", [128, 512], F32) for i in range(2)]
            qkv_mark = {}
            for e in range(3):
                for tb in range(4):
                    idx = e * 4 + tb

                    @self.on("pe")
                    def _(te, h=h, hb=hb, e=e, tb=tb, idx=idx):
                        clk.wait(te, "pe", "dpool", whb_dma[hb])
                        clk.wait(te, "pe", "dve", self.xnT_done)
                        if idx < 2:
                            clk.wait(te, "pe", "dve", prev_ot_mark)
                            clk.wait(te, "pe", "dve", getattr(self, "pe_gate_dve", 0))
                        if idx >= 2:
                            clk.wait(te, "pe", "dve", qkv_mark[idx - 2])
                        last = None
                        for j in range(16):
                            last = te.matmul(pq[idx % 2][:], whb[hb][:, 384 * j + 128 * e:384 * j + 128 * (e + 1)],
                                             self.xnT[j][:, 512 * tb:512 * (tb + 1)],
                                             start=(j == 0), stop=(j == 15))
                        clk.mark(last, "pe", 1)
                    pe_mark = clk.cnt["pe"]

                    @self.on("dve")
                    def _(ve, h=h, e=e, tb=tb, idx=idx, pe_mark=pe_mark):
                        clk.wait(ve, "dve", "pe", pe_mark)
                        if idx == 0 and dbg_QT is not None:
                            clk.wait(ve, "dve", "dact", clk.cnt["dact"])
                        dst = (QTh, KTh, VTh)[e]
                        bcol = bq_cols[:, 3 * h + e:3 * h + e + 1]
                        if e == 0:
                            inst = ve.tensor_scalar(dst[:, 512 * tb:512 * (tb + 1)], pq[idx % 2][:],
                                                    bcol, INV_SQRT_HD, ALU.add, ALU.mult)
                        else:
                            inst = ve.tensor_scalar(dst[:, 512 * tb:512 * (tb + 1)], pq[idx % 2][:],
                                                    bcol, None, ALU.add)
                        clk.mark(inst, "dve", 1)
                    qkv_mark[idx] = clk.cnt["dve"]
            qkv_done = clk.cnt["dve"]
            evac_done[h] = qkv_done
            self.free(f"pq{h}_1", f"pq{h}_0")
            if STAGE < 2:
                prev_ot_mark = clk.cnt["dve"]
                continue

            ptv = self.ps(f"ptv{h}", [128, 2048], BF16)

            @self.on("pe")
            def _(te, h=h, qkv_done=qkv_done):
                clk.wait(te, "pe", "dve", qkv_done)
                last = None
                for c in range(16):
                    last = te.transpose(ptv[:, 128 * c:128 * (c + 1)], VTh[:, 128 * c:128 * (c + 1)], ident_b[:])
                clk.mark(last, "pe", 1)
            ptv_mark = clk.cnt["pe"]

            @self.on("dve")
            def _(ve, ptv_mark=ptv_mark):
                clk.wait(ve, "dve", "pe", ptv_mark)
                clk.mark(ve.tensor_copy(vtm[:], ptv[:]), "dve", 1)
            vtm_done = clk.cnt["dve"]
            self.free(f"ptv{h}")

            pC = self.ps(f"pC{h}", [1, 512], F32)
            pB = self.ps(f"pB{h}", [128, 1], F32)
            for which, src in ((0, QTh), (1, KTh)):
                @self.on("act")
                def _(se, which=which, src=src, qkv_done=qkv_done):
                    clk.wait(se, "act", "dve", qkv_done)
                    if which == 1:
                        clk.wait(se, "act", "pe", clk.cnt["pe"])
                    clk.mark(se.activation(sqb[:], src[:], AF.Square), "act", 1)
                sq_mark = clk.cnt["act"]
                for s4 in range(4):
                    @self.on("pe")
                    def _(te, which=which, s4=s4, sq_mark=sq_mark, vtm_done=vtm_done):
                        clk.wait(te, "pe", "act", sq_mark)
                        clk.wait(te, "pe", "dve", vtm_done)
                        if which + s4 > 0:
                            clk.wait(te, "pe", "dve", clk.cnt["dve"])
                        clk.mark(te.matmul(pC[:], onescol_b[:], sqb[:, 512 * s4:512 * (s4 + 1)],
                                           start=True, stop=True), "pe", 1)

                    @self.on("dve")
                    def _(ve, which=which, s4=s4):
                        clk.wait(ve, "dve", "pe", clk.cnt["pe"])
                        clk.mark(ve.tensor_reduce(scrC[0:1, 4 * which + s4:4 * which + s4 + 1],
                                                  pC[0:1, :], AX.X, ALU.max), "dve", 1)

            @self.on("dve")
            def _(ve):
                ve.wait_ge(clk.sem["dve"], clk.cnt["dve"])
                ve.tensor_reduce(scrC[0:1, 8:9], scrC[0:1, 0:4], AX.X, ALU.max)
                clk.mark(ve.tensor_reduce(scrC[0:1, 9:10], scrC[0:1, 4:8], AX.X, ALU.max), "dve", 1)
                ve.wait_ge(clk.sem["dve"], clk.cnt["dve"])
                clk.mark(ve.tensor_mul(scrC[0:1, 10:11], scrC[0:1, 8:9], scrC[0:1, 9:10]), "dve", 1)

            @self.on("act")
            def _(se):
                clk.wait(se, "act", "dve", clk.cnt["dve"])
                clk.mark(se.activation(scrC[0:1, 11:12], scrC[0:1, 10:11], AF.Sqrt), "act", 1)

            @self.on("dve")
            def _(ve):
                clk.wait(ve, "dve", "act", clk.cnt["act"])
                clk.mark(ve.tensor_scalar(scrC[0:1, 12:13], scrC[0:1, 11:12], -1.0, None, ALU.mult), "dve", 1)
            negc_mark = clk.cnt["dve"]

            @self.on("pe")
            def _(te, negc_mark=negc_mark):
                clk.wait(te, "pe", "dve", negc_mark)
                clk.mark(te.matmul(pB[:], ones1_f[:], scrC[0:1, 12:13], start=True, stop=True), "pe", 1)

            @self.on("dve")
            def _(ve):
                clk.wait(ve, "dve", "pe", clk.cnt["pe"])
                clk.mark(ve.tensor_copy(nC[:], pB[:]), "dve", 1)
            nc_done = clk.cnt["dve"]
            self.free(f"pB{h}", f"pC{h}")
            if STAGE < 3:
                prev_ot_mark = clk.cnt["dve"]
                continue

            pS = [self.ps(f"pS{h}_{i}", [128, 1024], F32) for i in range(2)]
            pAV = self.ps(f"pAV{h}", [128, 1024], F32)
            pL = self.ps(f"pL{h}", [1, 1024], F32)
            prev_exp_last = 0
            prev_qh_ot = 0
            for qh in range(2):
                climit = 8 + 8 * qh
                exp_mark = {}
                av_mark = {}
                for c in range(climit):
                    cb = c % 2
                    qoff = max(0, 128 * c - 1024 * qh)
                    pieces = []
                    if qoff < 512:
                        pieces.append((qoff, 512))
                    pieces.append((max(qoff, 512), 1024))

                    @self.on("pe")
                    def _(te, h=h, qh=qh, c=c, cb=cb, pieces=pieces, vtm_done=vtm_done):
                        clk.wait(te, "pe", "dve", vtm_done)
                        if c < 2:
                            clk.wait(te, "pe", "dve", nc_done)
                            clk.wait(te, "pe", "act", prev_exp_last)
                            clk.wait(te, "pe", "dve", prev_qh_ot)
                        if c >= 2:
                            clk.wait(te, "pe", "act", exp_mark[c - 2])
                        for (a, b) in pieces:
                            clk.mark(te.matmul(pS[cb][:, a:b], KTh[:, 128 * c:128 * (c + 1)],
                                               QTh[:, 1024 * qh + a:1024 * qh + b], start=True, stop=True), "pe", 1)
                    s_mark = clk.cnt["pe"]

                    @self.on("act")
                    def _(se, c=c, cb=cb, qoff=qoff, s_mark=s_mark, nc_done=nc_done):
                        clk.wait(se, "act", "dve", nc_done)
                        clk.wait(se, "act", "pe", s_mark)
                        if c >= 2:
                            clk.wait(se, "act", "pe", av_mark[c - 2])
                        clk.mark(se.activation(Pb[cb][:, qoff:1024], pS[cb][:, qoff:1024], AF.Exp,
                                               bias=nC[:]), "act", 1)
                    exp_mark[c] = clk.cnt["act"]

                    dcol = 128 * c - 1024 * qh
                    masked = 0 <= dcol < 1024
                    if masked:
                        @self.on("dve")
                        def _(ve, cb=cb, dcol=dcol):
                            clk.wait(ve, "dve", "act", exp_mark[c])
                            clk.mark(ve.tensor_mul(Pb[cb][:, dcol:dcol + 128], Pb[cb][:, dcol:dcol + 128],
                                                   diag[:]), "dve", 1)
                        mask_mark = clk.cnt["dve"]
                    else:
                        mask_mark = None

                    @self.on("pe")
                    def _(te, h=h, qh=qh, c=c, cb=cb, pieces=pieces, climit=climit,
                          mask_mark=mask_mark):
                        clk.wait(te, "pe", "act", exp_mark[c])
                        if c == 0:
                            clk.wait(te, "pe", "dve", prev_qh_ot)
                        if mask_mark is not None:
                            clk.wait(te, "pe", "dve", mask_mark)
                        for (a, b) in pieces:
                            first = (c == 0)
                            lastc = (c == climit - 1) or (a == 0 and 128 * (c + 1) - 1024 * qh >= 512)
                            te.matmul(pAV[:, a:b], vtm[:, 128 * c:128 * (c + 1)], Pb[cb][:, a:b],
                                      start=first, stop=lastc)
                            last = te.matmul(pL[0:1, a:b], onescol_b[:], Pb[cb][:, a:b],
                                             start=first, stop=lastc)
                        clk.mark(last, "pe", 1)
                    av_mark[c] = clk.cnt["pe"]

                @self.on("dve")
                def _(ve, qh=qh):
                    clk.wait(ve, "dve", "pe", clk.cnt["pe"])
                    clk.mark(ve.reciprocal(linv[:], pL[0:1, :]), "dve", 1)
                linv_mark = clk.cnt["dve"]

                @self.on("pe")
                def _(te, linv_mark=linv_mark):
                    clk.wait(te, "pe", "dve", linv_mark)
                    te.matmul(pS[0][:, 0:512], ones1_f[:], linv[0:1, 0:512], start=True, stop=True)
                    clk.mark(te.matmul(pS[0][:, 512:1024], ones1_f[:], linv[0:1, 512:1024],
                                       start=True, stop=True), "pe", 1)

                @self.on("dve")
                def _(ve, h=h, qh=qh):
                    clk.wait(ve, "dve", "pe", clk.cnt["pe"])
                    clk.mark(ve.tensor_copy(binv[:], pS[0][:]), "dve", 1)
                    ve.wait_ge(clk.sem["dve"], clk.cnt["dve"])
                    clk.mark(ve.tensor_mul(self.OT[h][:, 1024 * qh:1024 * (qh + 1)], pAV[:], binv[:]), "dve", 1)
                prev_exp_last = exp_mark[climit - 1]
                prev_qh_ot = clk.cnt["dve"]
            prev_ot_mark = clk.cnt["dve"]
            self.free(f"pL{h}", f"pAV{h}", f"pS{h}_1", f"pS{h}_0")

            if dbg_QT is not None:
                @self.on("act")
                def _(se, h=h, qkv_done=qkv_done):
                    clk.wait(se, "act", "dve", qkv_done)
                    clk.mark(se.dma_start(out=dbg_QT[128 * h:128 * (h + 1), :], in_=QTh[:]), "dact", 16)
                    clk.mark(se.dma_start(out=dbg_KT[128 * h:128 * (h + 1), :], in_=KTh[:]), "dact", 16)

            if STAGE >= 4:
                self.head_extras(h)

        if STAGE >= 4 and SUBQ and not QINTER:
            self.quant_tail()
        self.OT_done = clk.cnt["dve"]
        if dbg_OT is not None:
            @self.on("act")
            def _(se):
                clk.wait(se, "act", "dve", self.OT_done)
                for h in range(8):
                    clk.mark(se.dma_start(out=dbg_OT[128 * h:128 * (h + 1), :], in_=self.OT[h][:]), "dact", 16)

        self.free("nC", "scrC", "linv", "binv", "Pb1", "Pb0", "sqb", "vtm", "VTh", "KTh", "QTh", "whb1", "whb0")
        if STAGE >= 4:
            self.free("stg7", "stg6", "stg5", "stg4", "stg3", "stg2", "stg1", "stg0",
                      "tern", "sht1", "sht0", "grow", "gacc")
        for j in reversed(range(16)):
            self.free(f"xnT{j}")

    def head_extras(self, h):
        if h == 0:
            for i in range(12):
                self.emit_gamma_slice(i // 8, i % 8)
        elif h == 1:
            for i in range(12, 24):
                self.emit_gamma_slice(i // 8, i % 8)
            if SUBF:
                self.emit_gamma_finishA()
        elif h == 2:
            if SUBF:
                self.emit_gamma_finishB()
        elif SUBQ and QINTER:
            if h == 3:
                for sl in range(8):
                    self.emit_quant_slice(0, sl)
            elif h == 4:
                self.emit_shard_out(0)
                for sl in range(4):
                    self.emit_quant_slice(1, sl)
            elif h == 5:
                for sl in range(4, 8):
                    self.emit_quant_slice(1, sl)
                self.emit_shard_out(1)
            elif h == 6:
                for sl in range(8):
                    self.emit_quant_slice(2, sl)
            elif h == 7:
                self.emit_shard_out(2)

    def quant_tail(self):
        for sh in range(3):
            for sl in range(8):
                self.emit_quant_slice(sh, sl)
            self.emit_shard_out(sh)

    def p_shard_quant(self):
        pass

    def p_woutpartial(self):
        if STAGE < 5:
            return
        clk = self.clk
        woutb = [self.sb(f"woutb{d}", [128, D], BF16) for d in range(8)]
        rss = [self.sb(f"rss{i}", [128, 512], F32) for i in range(2)]
        prs = [self.ps(f"prs{i}", [128, 512], F32) for i in range(2)]
        wtl = self.wout.rearrange("(a p) f -> a p f", p=128)

        @self.on("pool")
        def _(g):
            clk.wait(g, "pool", "dve", clk.cnt["dve"])
            clk.wait(g, "pool", "pe", clk.cnt["pe"])
            for d in range(8):
                clk.mark(g.dma_start(out=woutb[d][:], in_=wtl[d]), "dpool", 16)
        wload = clk.cnt["dpool"]

        ev_mark = {}
        st_mark = {}
        for tb in range(4):
            for dc in range(16):
                idx = tb * 16 + dc

                @self.on("pe")
                def _(te, tb=tb, dc=dc, idx=idx):
                    clk.wait(te, "pe", "dpool", wload)
                    clk.wait(te, "pe", "dve", self.OT_done)
                    if idx < 2:
                        clk.wait(te, "pe", "dve", getattr(self, "pe_gate_dve", 0))
                    if idx >= 2:
                        clk.wait(te, "pe", "dve", ev_mark[idx - 2])
                    last = None
                    for d in range(8):
                        last = te.matmul(prs[idx % 2][:], woutb[d][:, 128 * dc:128 * (dc + 1)],
                                         self.OT[d][:, 512 * tb:512 * (tb + 1)],
                                         start=(d == 0), stop=(d == 7))
                    clk.mark(last, "pe", 1)
                mm = clk.cnt["pe"]

                @self.on("dve")
                def _(ve, idx=idx, mm=mm):
                    clk.wait(ve, "dve", "pe", mm)
                    if idx >= 2:
                        clk.wait(ve, "dve", "dact", st_mark[idx - 2])
                    clk.mark(ve.tensor_copy(rss[idx % 2][:], prs[idx % 2][:]), "dve", 1)
                    ve.wait_ge(clk.sem["dve"], clk.cnt["dve"])
                    clk.mark(self.settle(ve), "dve", 1)
                ev_mark[idx] = clk.cnt["dve"]

                @self.on("act")
                def _(se, tb=tb, dc=dc, idx=idx):
                    clk.wait(se, "act", "dve", ev_mark[idx])
                    j = tb // 2
                    clk.mark(se.dma_start(
                        out=self.rs_in[2048 * j + 128 * dc:2048 * j + 128 * (dc + 1),
                                       512 * (tb % 2):512 * (tb % 2) + 512],
                        in_=rss[idx % 2][:]), "dact", 16)
                st_mark[idx] = clk.cnt["dact"]

        @self.on("pool")
        def _(g):
            clk.wait(g, "pool", "dact", clk.cnt["dact"])
            clk.mark(g.collective_compute("ReduceScatter", ALU.add, replica_groups=PAIRS,
                                          ins=[self.rs_in.opt()], outs=[self.rs_out.opt()]), "cc", 1)
        self.rs_mark = clk.cnt["cc"]
        self.free("prs1", "prs0")
        self.free("rss1", "rss0")
        self.free(*[f"woutb{d}" for d in reversed(range(8))])
        self.free(*[f"OT{j}" for j in reversed(range(8))])

        dbg = self.dbg("attn", [D, HT])
        if dbg is not None:
            @self.on("act")
            def _(se):
                clk.wait(se, "act", "cc", self.rs_mark)
                clk.mark(se.dma_start(out=dbg, in_=self.rs_out), "dact", 16)
        dbg2 = self.dbg("rsin", [2 * D, HT])
        if dbg2 is not None:
            @self.on("act")
            def _(se):
                clk.wait(se, "act", "cc", self.rs_mark)
                clk.mark(se.dma_start(out=dbg2, in_=self.rs_in), "dact", 16)

    def p_qT(self):
        if STAGE < 6:
            return
        clk = self.clk
        gxcol = self.bufs["gxcol"][1]
        gamcols = self.bufs["gamcols"][1]
        self.qT = [self.sb(f"qT{j}", [128, 1024], BF16) for j in range(16)]
        self.scg = self.sb("scg", [128, 8], F32)
        self.scv = self.sb("scv", [128, 8], F32)
        xq = [self.sb(f"xq{i}", [128, D], F32) for i in range(2)]
        xsq = self.sb("xsq", [128, D], F32)
        xnf = self.sb("xnf", [128, D], F32)
        q8 = self.sb("q8", [128, D], BF16)
        scr2 = self.sb("scr2", [128, 8], F32)
        mn2 = self.sb("mn2", [128, 1], F32)
        rs2 = self.sb("rs2", [128, 1], F32)
        xht = self.xhalf.rearrange("(a p) f -> a p f", p=128)
        cast_mark = {}

        for i in range(8):
            bi = i % 2

            @self.on("sp")
            def _(sp, i=i, bi=bi):
                if i < 2:
                    clk.wait(sp, "sp", "dve", clk.cnt["dve"])
                    clk.wait(sp, "sp", "pe", clk.cnt["pe"])
                else:
                    clk.wait(sp, "sp", "dve", cast_mark[i - 2])
                clk.mark(sp.dma_start(out=xq[bi][:], in_=xht[i]), "dsp", 16)
            dmam = clk.cnt["dsp"]

            @self.on("dve")
            def _(ve, i=i, bi=bi, dmam=dmam):
                clk.wait(ve, "dve", "dsp", dmam)
                self.ln_tile(ve, None, xq[bi], xsq, mn2, rs2, scr2)

            @self.on("act")
            def _(se):
                clk.wait(se, "act", "dve", clk.cnt["dve"])
                clk.mark(se.activation(rs2[:], scr2[:, 4:5], AF.Sqrt, bias=self.bufs["eps1"][1][:]), "act", 1)

            @self.on("dve")
            def _(ve, i=i, bi=bi):
                clk.wait(ve, "dve", "act", clk.cnt["act"])
                clk.mark(ve.reciprocal(rs2[:], rs2[:]), "dve", 1)
                ve.wait_ge(clk.sem["dve"], clk.cnt["dve"])
                clk.mark(ve.tensor_scalar(xnf[:], xq[bi][:], mn2[:], rs2[:], ALU.add, ALU.mult), "dve", 1)
                ve.wait_ge(clk.sem["dve"], clk.cnt["dve"])
                clk.mark(ve.tensor_reduce(gxcol[:, i:i + 1], xnf[:], AX.X, ALU.max,
                                          apply_absolute_value=True), "dve", 1)
                ve.wait_ge(clk.sem["dve"], clk.cnt["dve"])
                clk.mark(ve.tensor_scalar(gxcol[:, i:i + 1], gxcol[:, i:i + 1], 1e-5, None, ALU.max), "dve", 1)
                ve.wait_ge(clk.sem["dve"], clk.cnt["dve"])
                clk.mark(ve.reciprocal(scr2[:, 6:7], gxcol[:, i:i + 1]), "dve", 1)
                ve.wait_ge(clk.sem["dve"], clk.cnt["dve"])
                clk.mark(ve.tensor_scalar(scr2[:, 7:8], scr2[:, 6:7], 127.0, None, ALU.mult), "dve", 1)
                ve.wait_ge(clk.sem["dve"], clk.cnt["dve"])
                clk.mark(ve.tensor_scalar(xnf[:], xnf[:], scr2[:, 7:8], MAGIC, ALU.mult, ALU.add), "dve", 1)
                ve.wait_ge(clk.sem["dve"], clk.cnt["dve"])
                clk.wait(ve, "dve", "pe", getattr(self, "q8_free", 0))
                clk.mark(ve.tensor_scalar(q8[:], xnf[:], -MAGIC, None, ALU.add), "dve", 1)
            cast_mark[i] = clk.cnt["dve"]

            jobs = [(q8[:, 128 * j:128 * (j + 1)], self.qT[j][:, 128 * i:128 * (i + 1)])
                    for j in range(16)]
            self.transpose_batch(f"qt{i}", jobs, ("dve", cast_mark[i]))
            self.q8_free = clk.cnt["pe"]

        @self.on("dve")
        def _(ve):
            ve.wait_ge(clk.sem["dve"], clk.cnt["dve"])
            ve.tensor_scalar(self.scg[:], gxcol[:], gamcols[:, 0:1], None, ALU.mult)
            clk.mark(ve.tensor_scalar(self.scv[:], gxcol[:], gamcols[:, 1:2], None, ALU.mult), "dve", 1)
        self.qT_done = clk.cnt["dve"]
        self.free("rs2", "mn2", "scr2", "q8", "xnf", "xsq", "xq1", "xq0")

    def p_gateval(self):
        if STAGE < 7:
            return
        clk = self.clk
        gzcol = self.bufs["gzcol"][1]
        ones1_b = self.bufs["ones1_b"][1]
        wgg = [self.sb(f"wgg{i}", [128, H], BF16) for i in range(2)]
        wgv = [self.sb(f"wgv{i}", [128, H], BF16) for i in range(2)]
        gbrow = self.sb("gbrow", [1, H], BF16)
        vbrow = self.sb("vbrow", [1, H], BF16)
        gbb = self.sb("gbb", [128, 512], F32)
        vbb = self.sb("vbb", [128, 512], F32)
        gpre = self.sb("gpre", [128, 512], F32)
        gsil = self.sb("gsil", [128, 512], F32)
        vpre = self.sb("vpre", [128, 512], F32)
        zsl = [self.sb(f"zsl{i}", [128, 512], F32) for i in range(2)]
        zred = self.sb("zred", [128, 1], F32)
        pz = [self.ps(f"pz{i}", [128, 512], F32) for i in range(4)]
        pb2 = self.ps("pb2", [128, 1024], F32)
        agg_t = self.agg_out.rearrange("(a p) f -> a p f", p=128)
        agv_t = self.agv_out.rearrange("(a p) f -> a p f", p=128)

        @self.on("pool")
        def _(g):
            clk.wait(g, "pool", "dve", clk.cnt["dve"])
            clk.wait(g, "pool", "pe", clk.cnt["pe"])
            g.memset(gzcol[:], 0.0)
            clk.mark(g.dma_start(out=gbrow[:], in_=self.gbi), "dpool", 16)
            clk.mark(g.dma_start(out=vbrow[:], in_=self.vbi), "dpool", 16)
        brows = clk.cnt["dpool"]
        gz_init = clk.cnt["pool"]

        wload = {}
        chain_mark = {}
        evac_mark = {}
        zdma_mark = {}
        for g_ in range(16):
            gb_ = g_ % 2

            @self.on("sp")
            def _(sp, g_=g_, gb_=gb_):
                clk.wait(sp, "sp", "cc", self.ag_mark["v"])
                if g_ < 2:
                    clk.wait(sp, "sp", "dve", clk.cnt["dve"])
                else:
                    clk.wait(sp, "sp", "pe", chain_mark[(g_ - 2, 7, "v")])
                last = None
                for d in range(16):
                    last = sp.dma_start(out=wgg[gb_][:, 512 * d:512 * (d + 1)],
                                        in_=agg_t[d][:, 512 * g_:512 * (g_ + 1)])
                    clk.mark(last, "dsp", 16)
                for d in range(16):
                    last = sp.dma_start(out=wgv[gb_][:, 512 * d:512 * (d + 1)],
                                        in_=agv_t[d][:, 512 * g_:512 * (g_ + 1)])
                    clk.mark(last, "dsp", 16)
            wload[g_] = clk.cnt["dsp"]

            @self.on("pe")
            def _(te, g_=g_):
                clk.wait(te, "pe", "dpool", brows)
                if g_ >= 1:
                    clk.wait(te, "pe", "dve", evac_mark[(g_ - 1, 0, "bias")])
                te.matmul(pb2[:, 0:512], ones1_b[:], gbrow[0:1, 512 * g_:512 * (g_ + 1)], start=True, stop=True)
                clk.mark(te.matmul(pb2[:, 512:1024], ones1_b[:], vbrow[0:1, 512 * g_:512 * (g_ + 1)],
                                   start=True, stop=True), "pe", 1)
            bias_pe = clk.cnt["pe"]

            @self.on("dve")
            def _(ve, g_=g_, bias_pe=bias_pe):
                clk.wait(ve, "dve", "pe", bias_pe)
                ve.tensor_copy(gbb[:], pb2[:, 0:512])
                clk.mark(ve.tensor_copy(vbb[:], pb2[:, 512:1024]), "dve", 1)
            evac_mark[(g_, 0, "bias")] = clk.cnt["dve"]
            bias_dve = clk.cnt["dve"]

            for tc in range(8):
                ci = 2 * (8 * g_ + tc)
                for which in ("g", "v"):
                    pzi = (ci + (0 if which == "g" else 1)) % 4

                    @self.on("pe")
                    def _(te, g_=g_, gb_=gb_, tc=tc, which=which, pzi=pzi):
                        clk.wait(te, "pe", "dsp", wload[g_])
                        clk.wait(te, "pe", "dve", self.qT_done)
                        prev = evac_mark.get((g_, tc - 2, which)) or evac_mark.get((g_ - 1, 6 + tc, which))
                        if prev:
                            clk.wait(te, "pe", "dve", prev)
                        wsrc = wgg[gb_] if which == "g" else wgv[gb_]
                        last = None
                        for d in range(16):
                            last = te.matmul(pz[pzi][:], self.qT[d][:, 128 * tc:128 * (tc + 1)],
                                             wsrc[:, 512 * d:512 * (d + 1)],
                                             start=(d == 0), stop=(d == 15))
                        clk.mark(last, "pe", 1)
                    chain_mark[(g_, tc, which)] = clk.cnt["pe"]

                    if which == "g":
                        @self.on("dve")
                        def _(ve, g_=g_, tc=tc, pzi=pzi):
                            clk.wait(ve, "dve", "pe", chain_mark[(g_, tc, "g")])
                            ve.wait_ge(clk.sem["dve"], bias_dve)
                            clk.mark(ve.scalar_tensor_tensor(gpre[:], pz[pzi][:], self.scg[:, tc:tc + 1],
                                                             gbb[:], ALU.mult, ALU.add), "dve", 1)
                        evac_mark[(g_, tc, "g")] = clk.cnt["dve"]

                        @self.on("act")
                        def _(se, g_=g_, tc=tc):
                            clk.wait(se, "act", "dve", evac_mark[(g_, tc, "g")])
                            clk.mark(se.activation(gsil[:], gpre[:], AF.Silu), "act", 1)
                        evac_mark[(g_, tc, "sil")] = clk.cnt["act"]
                    else:
                        @self.on("dve")
                        def _(ve, g_=g_, tc=tc, pzi=pzi):
                            clk.wait(ve, "dve", "pe", chain_mark[(g_, tc, "v")])
                            prevz = zdma_mark.get((g_, tc - 2)) or zdma_mark.get((g_ - 1, 6 + tc))
                            if prevz:
                                clk.wait(ve, "dve", "dact", prevz)
                            clk.mark(ve.scalar_tensor_tensor(vpre[:], pz[pzi][:], self.scv[:, tc:tc + 1],
                                                             vbb[:], ALU.mult, ALU.add), "dve", 1)
                            clk.wait(ve, "dve", "act", evac_mark[(g_, tc, "sil")])
                            ve.wait_ge(clk.sem["dve"], clk.cnt["dve"])
                            clk.mark(ve.tensor_mul(zsl[tc % 2][:], gsil[:], vpre[:]), "dve", 1)
                            ve.wait_ge(clk.sem["dve"], clk.cnt["dve"])
                            clk.mark(ve.tensor_reduce(zred[:], zsl[tc % 2][:], AX.X, ALU.max,
                                                      apply_absolute_value=True), "dve", 1)
                            ve.wait_ge(clk.sem["dve"], clk.cnt["dve"])
                            if g_ == 0:
                                clk.wait(ve, "dve", "pool", gz_init)
                            clk.mark(ve.tensor_max(gzcol[:, tc:tc + 1], gzcol[:, tc:tc + 1], zred[:]), "dve", 1)
                            ve.wait_ge(clk.sem["dve"], clk.cnt["dve"])
                            clk.mark(self.settle(ve), "dve", 1)
                        evac_mark[(g_, tc, "v")] = clk.cnt["dve"]

                        @self.on("act")
                        def _(se, g_=g_, tc=tc):
                            clk.wait(se, "act", "dve", evac_mark[(g_, tc, "v")])
                            clk.mark(se.dma_start(
                                out=self.zdr[128 * tc:128 * (tc + 1), 512 * g_:512 * (g_ + 1)],
                                in_=zsl[tc % 2][:]), "dact", 16)
                        zdma_mark[(g_, tc)] = clk.cnt["dact"]

        self.z_done = clk.cnt["dact"]
        self.free("pb2", "pz3", "pz2", "pz1", "pz0")
        self.free("zred", "zsl1", "zsl0", "vpre", "gsil", "gpre", "vbb", "gbb",
                  "vbrow", "gbrow", "wgv1", "wgv0", "wgg1", "wgg0")
        self.free("scv", "scg")
        self.free(*[f"qT{j}" for j in reversed(range(16))])

        dbg = self.dbg("z", [HT, H])
        if dbg is not None:
            @self.on("pool")
            def _(g):
                clk.wait(g, "pool", "dact", self.z_done)
                clk.mark(g.dma_start(out=dbg, in_=self.zdr), "dpool", 16)

    def p_qzT(self):
        if STAGE < 8:
            return
        clk = self.clk
        gzcol = self.bufs["gzcol"][1]
        gamcols = self.bufs["gamcols"][1]
        ident_f = self.bufs["ident_f"][1]
        ones1_f = self.bufs["ones1_f"][1]
        self.qzT = [self.sb(f"qzT{j}", [128, 1024], BF16) for j in range(64)]
        self.sob = self.sb("sob", [128, 1024], F32)
        sorow = self.sb("sorow", [1, 1024], F32)
        socol = self.sb("socol", [128, 8], F32)
        szcol = self.sb("szcol", [128, 8], F32)
        zl = [self.sb(f"zl{i}", [128, 4096], F32) for i in range(2)]
        qz8 = self.sb("qz8", [128, 4096], BF16)
        pso = self.ps("pso", [128, 1024], F32)

        @self.on("dve")
        def _(ve):
            ve.wait_ge(clk.sem["dve"], clk.cnt["dve"])
            clk.mark(ve.tensor_scalar(gzcol[:], gzcol[:], 1e-5, None, ALU.max), "dve", 1)
            ve.wait_ge(clk.sem["dve"], clk.cnt["dve"])
            ve.tensor_scalar(socol[:], gzcol[:], gamcols[:, 2:3], None, ALU.mult)
            clk.mark(ve.reciprocal(szcol[:], gzcol[:]), "dve", 1)
            ve.wait_ge(clk.sem["dve"], clk.cnt["dve"])
            clk.mark(ve.tensor_scalar(szcol[:], szcol[:], 127.0, None, ALU.mult), "dve", 1)
        sz_mark = clk.cnt["dve"]

        @self.on("pe")
        def _(te):
            clk.wait(te, "pe", "dve", sz_mark)
            last = None
            for i in range(8):
                last = te.matmul(pso[0:1, 128 * i:128 * (i + 1)], socol[:, i:i + 1], ident_f[:],
                                 start=True, stop=True)
            clk.mark(last, "pe", 1)

        @self.on("dve")
        def _(ve):
            clk.wait(ve, "dve", "pe", clk.cnt["pe"])
            clk.mark(ve.tensor_copy(sorow[:], pso[0:1, :]), "dve", 1)
        sorow_mark = clk.cnt["dve"]

        @self.on("pe")
        def _(te):
            clk.wait(te, "pe", "dve", sorow_mark)
            te.matmul(pso[:, 0:512], ones1_f[:], sorow[0:1, 0:512], start=True, stop=True)
            clk.mark(te.matmul(pso[:, 512:1024], ones1_f[:], sorow[0:1, 512:1024], start=True, stop=True), "pe", 1)

        @self.on("dve")
        def _(ve):
            clk.wait(ve, "dve", "pe", clk.cnt["pe"])
            clk.mark(ve.tensor_copy(self.sob[:], pso[:]), "dve", 1)
        self.sob_done = clk.cnt["dve"]

        dbgsob = self.dbg("sob", [128, 1024])
        if dbgsob is not None:
            @self.on("act")
            def _(se):
                clk.wait(se, "act", "dve", self.sob_done)
                clk.mark(se.dma_start(out=dbgsob, in_=self.sob[:]), "dact", 16)
        dbgz = self.dbg("gz", [128, 16])
        if dbgz is not None:
            @self.on("act")
            def _(se):
                clk.wait(se, "act", "dve", clk.cnt["dve"])
                clk.mark(se.dma_start(out=dbgz[:, 0:8], in_=gzcol[:]), "dact", 16)
                clk.mark(se.dma_start(out=dbgz[:, 8:16], in_=szcol[:]), "dact", 16)

        qz_mark = {}
        for i in range(8):
            for half in range(2):
                hb = (2 * i + half) % 2

                @self.on("sp")
                def _(sp, i=i, half=half, hb=hb):
                    clk.wait(sp, "sp", "dact", self.z_done)
                    prev = qz_mark.get((i, half - 2)) or qz_mark.get((i - 1, half))
                    if prev:
                        clk.wait(sp, "sp", "dve", prev)
                    clk.mark(sp.dma_start(out=zl[hb][:],
                                          in_=self.zdr[128 * i:128 * (i + 1),
                                                       4096 * half:4096 * (half + 1)]), "dsp", 16)
                dmam = clk.cnt["dsp"]

                @self.on("dve")
                def _(ve, i=i, half=half, hb=hb, dmam=dmam):
                    clk.wait(ve, "dve", "dsp", dmam)
                    ve.wait_ge(clk.sem["dve"], sz_mark)
                    clk.mark(ve.tensor_scalar(zl[hb][:], zl[hb][:], szcol[:, i:i + 1], MAGIC,
                                              ALU.mult, ALU.add), "dve", 1)
                    ve.wait_ge(clk.sem["dve"], clk.cnt["dve"])
                    clk.wait(ve, "dve", "pe", getattr(self, "qz8_free", 0))
                    clk.mark(ve.tensor_scalar(qz8[:], zl[hb][:], -MAGIC, None, ALU.add), "dve", 1)
                qz_mark[(i, half)] = clk.cnt["dve"]

                jobs = [(qz8[:, 128 * j:128 * (j + 1)],
                         self.qzT[32 * half + j][:, 128 * i:128 * (i + 1)]) for j in range(32)]
                self.transpose_batch(f"qz{i}_{half}", jobs, ("dve", qz_mark[(i, half)]))
                self.qz8_free = clk.cnt["pe"]
        self.qzT_done = clk.cnt["dve"]
        dbgq = self.dbg("qzT", [H, 1024], BF16)
        if dbgq is not None:
            @self.on("act")
            def _(se):
                clk.wait(se, "act", "dve", self.qzT_done)
                for j in range(64):
                    clk.mark(se.dma_start(out=dbgq[128 * j:128 * (j + 1), :], in_=self.qzT[j][:]), "dact", 16)
        self.free("pso")
        self.free("qz8", "zl1", "zl0", "szcol", "socol", "sorow")

    def p_outproj(self):
        if STAGE < 8:
            return
        clk = self.clk
        ident_f = self.bufs["ident_f"][1]
        bofo_col = self.bufs["bofo_col"][1]
        otbig = self.sb("otbig", [128, 64 * 256], BF16)
        atile = [self.sb(f"atile{i}", [128, 512], F32) for i in range(4)]
        xtile = [self.sb(f"xtile{i}", [128, 256], F32) for i in range(8)]
        resT = [self.sb(f"resT{i}", [128, 1024], F32) for i in range(2)]
        ystg = [self.sb(f"ystg{i}", [128, 256], F32) for i in range(2)]
        po = [self.ps(f"po{i}", [128, 512], F32) for i in range(4)]
        ptr2 = [self.ps(f"ptr2_{i}", [128, 512], F32) for i in range(2)]
        ago_t = self.ago_out.rearrange("(a p) f -> a p f", p=128)
        dgs = [(2 * i, 2) for i in range(8)]

        mm_mark = {}
        res_mark = {}
        y_mark = {}
        yadd = {}
        for gi, (d0, nd) in enumerate(dgs):
            @self.on("sp")
            def _(sp, gi=gi, d0=d0, nd=nd):
                clk.wait(sp, "sp", "cc", self.ag_mark["o"])
                if gi == 0:
                    clk.wait(sp, "sp", "dve", clk.cnt["dve"])
                else:
                    clk.wait(sp, "sp", "pe", mm_mark[(gi - 1, 63)])
                for hh in range(64):
                    clk.mark(sp.dma_start(out=otbig[:, 256 * hh:256 * hh + 128 * nd],
                                          in_=ago_t[hh][:, 128 * d0:128 * (d0 + nd)]), "dsp", 16)
            ot_dma = clk.cnt["dsp"]

            @self.on("sp")
            def _(sp, gi=gi, d0=d0, nd=nd):
                clk.wait(sp, "sp", "cc", self.rs_mark)
                if gi >= 1:
                    clk.wait(sp, "sp", "dve", res_mark[gi - 1])
                    clk.wait(sp, "sp", "dve", yadd[(gi - 1, 7)])
                for di in range(nd):
                    for tbk in range(2):
                        dc = d0 + di
                        clk.mark(sp.dma_start(out=atile[2 * di + tbk][:],
                                              in_=self.rs_out[128 * dc:128 * (dc + 1),
                                                              512 * tbk:512 * (tbk + 1)]), "dsp", 16)
                for t in range(8):
                    clk.mark(sp.dma_start(out=xtile[t][:, 0:128 * nd],
                                          in_=self.xhalf[128 * t:128 * (t + 1),
                                                         128 * d0:128 * (d0 + nd)]), "dsp", 16)
            at_dma = clk.cnt["dsp"]

            for hh in range(64):
                @self.on("pe")
                def _(te, gi=gi, d0=d0, nd=nd, hh=hh):
                    if hh == 0:
                        clk.wait(te, "pe", "dsp", ot_dma)
                        clk.wait(te, "pe", "dve", self.qzT_done)
                        if gi > 0:
                            clk.wait(te, "pe", "dve", res_mark[gi - 1])
                    last = None
                    for di in range(nd):
                        for tbk in range(2):
                            last = te.matmul(po[2 * di + tbk][:], otbig[:, 256 * hh + 128 * di:256 * hh + 128 * (di + 1)],
                                             self.qzT[hh][:, 512 * tbk:512 * (tbk + 1)],
                                             start=(hh == 0), stop=(hh == 63))
                    if hh == 63:
                        last = self.fence(te, po[2 * nd - 1][:], 512)
                    clk.mark(last, "pe", 1)
                mm_mark[(gi, hh)] = clk.cnt["pe"]

            for di in range(nd):
                for tbk in range(2):
                    @self.on("dve")
                    def _(ve, gi=gi, d0=d0, di=di, tbk=tbk):
                        clk.wait(ve, "dve", "pe", mm_mark[(gi, 63)])
                        clk.wait(ve, "dve", "dsp", at_dma)
                        ve.wait_ge(clk.sem["dve"], self.sob_done)
                        dc = d0 + di
                        sl = resT[di][:, 512 * tbk:512 * (tbk + 1)]
                        clk.mark(ve.tensor_mul(sl, po[2 * di + tbk][:], self.sob[:, 512 * tbk:512 * (tbk + 1)]),
                                 "dve", 1)
                        ve.wait_ge(clk.sem["dve"], clk.cnt["dve"])
                        clk.mark(ve.scalar_tensor_tensor(sl, sl, bofo_col[:, dc:dc + 1],
                                                         atile[2 * di + tbk][:],
                                                         ALU.add, ALU.add), "dve", 1)

            @self.on("dve")
            def _(ve):
                ve.wait_ge(clk.sem["dve"], clk.cnt["dve"])
                clk.mark(self.settle(ve), "dve", 1)
            res_mark[gi] = clk.cnt["dve"]

            dbgr = self.dbg("resT", [D, HT])
            if dbgr is not None:
                @self.on("act")
                def _(se, gi=gi, d0=d0, nd=nd):
                    clk.wait(se, "act", "dve", res_mark[gi])
                    for di in range(nd):
                        clk.mark(se.dma_start(out=dbgr[128 * (d0 + di):128 * (d0 + di + 1), :],
                                              in_=resT[di][:]), "dact", 16)

            for t in range(8):
                tb2 = t % 2

                @self.on("pe")
                def _(te, gi=gi, t=t, tb2=tb2, nd=nd):
                    clk.wait(te, "pe", "dve", res_mark[gi])
                    prev = yadd.get((gi, t - 2)) or yadd.get((gi - 1, 6 + t))
                    if prev:
                        clk.wait(te, "pe", "dve", prev)
                    last = None
                    for di in range(nd):
                        last = te.matmul(ptr2[tb2][:, 128 * di:128 * (di + 1)],
                                         resT[di][:, 128 * t:128 * (t + 1)], ident_f[:],
                                         start=True, stop=True)
                    clk.mark(self.fence(te, ptr2[tb2][:, 128 * (nd - 1):128 * nd], 128), "pe", 1)
                trm = clk.cnt["pe"]

                @self.on("dve")
                def _(ve, gi=gi, t=t, tb2=tb2, nd=nd, trm=trm):
                    clk.wait(ve, "dve", "pe", trm)
                    clk.wait(ve, "dve", "dsp", at_dma)
                    prev = y_mark.get((gi, t - 2)) or y_mark.get((gi - 1, 6 + t))
                    if prev:
                        clk.wait(ve, "dve", "dact", prev)
                    clk.mark(ve.tensor_add(ystg[tb2][:, 0:128 * nd], ptr2[tb2][:, 0:128 * nd],
                                           xtile[t][:, 0:128 * nd]), "dve", 1)
                    ve.wait_ge(clk.sem["dve"], clk.cnt["dve"])
                    clk.mark(self.settle(ve), "dve", 1)
                yadd[(gi, t)] = clk.cnt["dve"]

                @self.on("act")
                def _(se, gi=gi, d0=d0, nd=nd, t=t, tb2=tb2):
                    clk.wait(se, "act", "dve", yadd[(gi, t)])
                    clk.mark(se.dma_start(out=self.y[128 * t:128 * (t + 1), 128 * d0:128 * (d0 + nd)],
                                          in_=ystg[tb2][:, 0:128 * nd]), "dact", 16)
                y_mark[(gi, t)] = clk.cnt["dact"]

        self.free("ptr2_1", "ptr2_0", "po3", "po2", "po1", "po0")
        self.free("ystg1", "ystg0", "resT1", "resT0", "xtile7", "xtile6", "xtile5", "xtile4", "xtile3", "xtile2", "xtile1", "xtile0",
                  "atile3", "atile2", "atile1", "atile0", "otbig")
        self.free("sob", *[f"qzT{j}" for j in reversed(range(64))])

    def p_finish(self):
        clk = self.clk

        if STAGE < 8:
            yst = self.sb("yst", [128, D], F32)

            @self.on("pool")
            def _(g):
                clk.wait(g, "pool", "dve", clk.cnt["dve"])
                clk.wait(g, "pool", "act", clk.cnt["act"])
                clk.wait(g, "pool", "pe", clk.cnt["pe"])
                clk.mark(g.memset(yst[:], 0.0), "pool", 1)

            @self.on("act")
            def _(se):
                clk.wait(se, "act", "pool", clk.cnt["pool"])
                ytiles = self.y.rearrange("(a p) f -> a p f", p=128)
                for i in range(8):
                    clk.mark(se.dma_start(out=ytiles[i], in_=yst[:]), "dact", 16)

        @self.on("act")
        def _(se):
            clk.wait(se, "act", "dact", clk.cnt["dact"])
            clk.wait(se, "act", "dsp", clk.cnt["dsp"])
            clk.wait(se, "act", "dpool", clk.cnt["dpool"])
            clk.wait(se, "act", "cc", clk.cnt["cc"])


def shard_inputs(ins):
    x = np.ascontiguousarray(ins["L0_attn_x"])
    Wqkv = np.ascontiguousarray(ins["L0_attn_W_qkv"])
    bqkv = np.ascontiguousarray(ins["L0_attn_b_qkv"]).reshape(1, -1)
    Wout = np.ascontiguousarray(ins["L0_attn_W_out"])
    bout = np.ascontiguousarray(ins["L0_attn_b_out"]).reshape(1, -1)
    gw = np.ascontiguousarray(ins["L0_ffn_gate_w"])
    gb = np.ascontiguousarray(ins["L0_ffn_gate_b"]).reshape(1, -1)
    vw = np.ascontiguousarray(ins["L0_ffn_val_w"])
    vb = np.ascontiguousarray(ins["L0_ffn_val_b"]).reshape(1, -1)
    ow = np.ascontiguousarray(ins["L0_ffn_out_w"])
    ob = np.ascontiguousarray(ins["L0_ffn_out_b"]).reshape(1, -1)

    maps = []
    for c in range(8):
        b, p = c // 2, c % 2
        hs = slice(1024 * p, 1024 * (p + 1))
        m = {
            "xseq": x[b],
            "xhalf": np.ascontiguousarray(x[b, 1024 * p:1024 * (p + 1)]),
            "wqkv": np.ascontiguousarray(np.concatenate(
                [Wqkv[:, base + 128 * (8 * p + h):base + 128 * (8 * p + h + 1)]
                 for h in range(8) for base in (0, 2048, 4096)], axis=1)),
            "bqkv": np.ascontiguousarray(np.concatenate(
                [bqkv[:, base + 128 * (8 * p + h):base + 128 * (8 * p + h + 1)]
                 for h in range(8) for base in (0, 2048, 4096)], axis=1)),
            "wout": np.ascontiguousarray(Wout[hs, :]),
            "bout": bout,
            "fob": ob,
            "gbi": gb,
            "vbi": vb,
            "gsh": np.ascontiguousarray(gw[:, 256 * c:256 * (c + 1)]),
            "vsh": np.ascontiguousarray(vw[:, 256 * c:256 * (c + 1)]),
            "osh": np.ascontiguousarray(ow[:, 1024 * c:1024 * (c + 1)]),
        }
        maps.append(m)
    return maps


_CACHED = {}


def get_nc():
    key = tuple(sorted(DBG))
    if key not in _CACHED:
        _CACHED[key] = KB()
        _CACHED[key].nc_built = _CACHED[key].build()
    return _CACHED[key]


def kernel(**inputs):
    kb = get_nc()
    maps = shard_inputs(inputs)
    res = run_bass_kernel_spmd(kb.nc_built, maps, core_ids=list(range(8)))
    out = np.empty((4, 2048, 2048), np.float32)
    for c in range(8):
        b, p = c // 2, c % 2
        out[b, 1024 * p:1024 * (p + 1)] = res.results[c]["y"]
    kernel.last_results = res.results
    return out


# version 09 (final)
# speedup vs baseline: 1.0383x; 1.0383x over previous
import contextlib
import numpy as np

import concourse.bass as bass
import concourse.mybir as mybir
from concourse.bass_utils import run_bass_kernel_spmd

F32 = mybir.dt.float32
BF16 = mybir.dt.bfloat16
AF = mybir.ActivationFunctionType
ALU = mybir.AluOpType
AX = mybir.AxisListType

MAGIC = 12582912.0
GNUDGE = 0.0
INV_SQRT_HD = 0.08838834764831845
S = 2048
HT = 1024
D = 2048
H = 8192
NHC = 8
PAIRS = [[0, 1], [2, 3], [4, 5], [6, 7]]
ALL8 = [list(range(8))]

DBG = set()
STAGE = 99
NHEADS = 8
SUBQ = True
SUBF = True
QINTER = True


class Clock:

    def __init__(self, nc, ctx):
        self.nc = nc
        self.sem = {}
        self.cnt = {}
        self.waited = {}
        for name in ("pe", "dve", "act", "pool", "sp", "dsp", "dact", "dpool", "cc"):
            self.sem[name] = ctx.enter_context(nc.semaphore("sem_" + name))
            self.cnt[name] = 0

    def mark(self, inst, prod, n=1):
        self.cnt[prod] += n
        inst.then_inc(self.sem[prod], n)
        return self.cnt[prod]

    def tick(self, prod, n=1):
        self.cnt[prod] += n
        return self.cnt[prod]

    def wait(self, eng, consumer, prod, val=None):
        val = self.cnt[prod] if val is None else val
        w = self.waited.setdefault(consumer, {})
        if w.get(prod, -1) >= val:
            return
        eng.wait_ge(self.sem[prod], val)
        w[prod] = val


class KB:

    ENG = {"pe": "tensor", "dve": "vector", "act": "scalar", "pool": "gpsimd", "sp": "sync"}
    DQ = {"sp": "dsp", "act": "dact", "pool": "dpool"}

    def __init__(self):
        nc = bass.Bass(trn_type="TRN2", target_bir_lowering=False)
        self.nc = nc
        self.ctx = contextlib.ExitStack()
        self.bufs = {}
        self.dbg_outs = {}

        dt = lambda n, sh, k: nc.dram_tensor(n, sh, F32, kind=k).ap()
        self.xseq = dt("xseq", [S, D], "ExternalInput")
        self.xhalf = dt("xhalf", [HT, D], "ExternalInput")
        self.wqkv = dt("wqkv", [D, 3 * NHC * 128], "ExternalInput")
        self.bqkv = dt("bqkv", [1, 3 * NHC * 128], "ExternalInput")
        self.wout = dt("wout", [NHC * 128, D], "ExternalInput")
        self.bout = dt("bout", [1, D], "ExternalInput")
        self.fob = dt("fob", [1, D], "ExternalInput")
        self.gbi = dt("gbi", [1, H], "ExternalInput")
        self.vbi = dt("vbi", [1, H], "ExternalInput")
        self.gsh = dt("gsh", [H, 256], "ExternalInput")
        self.vsh = dt("vsh", [H, 256], "ExternalInput")
        self.osh = dt("osh", [D, 1024], "ExternalInput")
        self.y = dt("y", [HT, D], "ExternalOutput")
        self.gam_in = nc.dram_tensor("gam_in", [1, 4], F32).ap()
        self.gam_out = nc.dram_tensor("gam_out", [1, 4], F32).ap()
        self.agg_in = nc.dram_tensor("agg_in", [256, H], BF16).ap()
        self.agg_out = nc.dram_tensor("agg_out", [D, H], BF16).ap()
        self.agv_in = nc.dram_tensor("agv_in", [256, H], BF16).ap()
        self.agv_out = nc.dram_tensor("agv_out", [D, H], BF16).ap()
        self.ago_in = nc.dram_tensor("ago_in", [1024, D], BF16).ap()
        self.ago_out = nc.dram_tensor("ago_out", [H, D], BF16).ap()
        self.rs_in = nc.dram_tensor("rs_in", [2 * D, HT], F32).ap()
        self.rs_out = nc.dram_tensor("rs_out", [D, HT], F32).ap()
        self.zdr = nc.dram_tensor("zdr", [HT, H], F32).ap()

    def sb(self, name, shape, dtype):
        cm = self.nc.sbuf_tensor(name, shape, dtype)
        t = cm.__enter__()
        self.bufs[name] = (cm, t)
        return t

    def ps(self, name, shape, dtype=F32):
        cm = self.nc.psum_tensor(name, shape, dtype)
        t = cm.__enter__()
        self.bufs[name] = (cm, t)
        return t

    def free(self, *names):
        for name in names:
            cm, _ = self.bufs.pop(name)
            cm.__exit__(None, None, None)

    def on(self, eng):
        def deco(f):
            getattr(self.blk, self.ENG[eng])(f)
        return deco

    def dbg(self, name, shape, dtype=F32):
        if name not in DBG:
            return None
        if name not in self.dbg_outs:
            self.dbg_outs[name] = self.nc.dram_tensor("dbg_" + name, shape, dtype,
                                                      kind="ExternalOutput").ap()
        return self.dbg_outs[name]

    def build(self):
        nc = self.nc
        with self.ctx:
            self.clk = Clock(nc, self.ctx)
            self.blk = self.ctx.enter_context(nc.Block())
            self.p_consts()
            self.p_ln()
            self.p_gamma_partials()
            self.p_heads()
            self.p_shard_quant()
            self.barrier("bh")
            self.p_woutqT()
            self.barrier("bq")
            self.p_gateval()
            self.barrier("bg")
            self.p_qzT()
            self.barrier("bz")
            self.p_outproj()
            self.p_finish()
        return nc

    def p_consts(self):
        clk = self.clk
        ident_b = self.sb("ident_b", [128, 128], BF16)
        ident_f = self.sb("ident_f", [128, 128], F32)
        diag = self.sb("diag", [128, 128], BF16)
        ones1_f = self.sb("ones1_f", [1, 128], F32)
        ones1_b = self.sb("ones1_b", [1, 128], BF16)
        onescol_f = self.sb("onescol_f", [128, 1], F32)
        onescol_b = self.sb("onescol_b", [128, 1], BF16)
        eps1 = self.sb("eps1", [128, 1], F32)
        bq_cols = self.sb("bq_cols", [128, 24], F32)
        bofo_col = self.sb("bofo_col", [128, 16], F32)
        gamcols = self.sb("gamcols", [128, 8], F32)
        gxcol = self.sb("gxcol", [128, 8], F32)
        gzcol = self.sb("gzcol", [128, 8], F32)
        zrow_b = self.sb("zrow_b", [1, 512], BF16)
        self.settle_col = self.sb("settle_col", [1, 8], F32)
        bq_row = self.sb("bq_row", [1, 3072], F32)
        bo_row = self.sb("bo_row", [1, D], F32)
        fo_row = self.sb("fo_row", [1, D], F32)
        pmisc = self.ps("pmisc", [128, 512], F32)

        @self.on("sp")
        def _(sp):
            clk.mark(sp.dma_start(out=bq_row[:], in_=self.bqkv), "dsp", 16)
            clk.mark(sp.dma_start(out=bo_row[:], in_=self.bout), "dsp", 16)
            clk.mark(sp.dma_start(out=fo_row[:], in_=self.fob), "dsp", 16)
        self.dma_done_consts = clk.cnt["dsp"]

        @self.on("pool")
        def _(g):
            g.memset(ident_b[:], 0.0)
            g.affine_select(out=ident_b[:], in_=ident_b[:], compare_op=ALU.not_equal,
                            fill=1.0, base=0, pattern=[[-1, 128]], channel_multiplier=1)
            g.memset(ident_f[:], 0.0)
            g.affine_select(out=ident_f[:], in_=ident_f[:], compare_op=ALU.not_equal,
                            fill=1.0, base=0, pattern=[[-1, 128]], channel_multiplier=1)
            g.memset(diag[:], 1.0)
            g.affine_select(out=diag[:], in_=diag[:], compare_op=ALU.is_ge,
                            fill=0.0, base=0, pattern=[[1, 128]], channel_multiplier=-1)
            g.memset(ones1_f[:], 1.0)
            g.memset(ones1_b[:], 1.0)
            g.memset(onescol_f[:], 1.0)
            g.memset(onescol_b[:], 1.0)
            g.memset(zrow_b[:], 0.0)
            clk.mark(g.memset(eps1[:], 1e-5), "pool", 1)
        self.consts_ready = clk.cnt["pool"]

        @self.on("pe")
        def _(te):
            clk.wait(te, "pe", "dsp", self.dma_done_consts)
            clk.wait(te, "pe", "pool", self.consts_ready)
            one11 = ones1_f[0:1, 0:1]
            last = None
            for e in range(24):
                last = te.matmul(pmisc[:, e:e + 1], bq_row[0:1, 128 * e:128 * (e + 1)], one11, start=True, stop=True)
            for e in range(16):
                last = te.matmul(pmisc[:, 24 + e:25 + e], bo_row[0:1, 128 * e:128 * (e + 1)], one11, start=True, stop=True)
            for e in range(16):
                last = te.matmul(pmisc[:, 40 + e:41 + e], fo_row[0:1, 128 * e:128 * (e + 1)], one11, start=True, stop=True)
            clk.mark(last, "pe", 1)

        @self.on("dve")
        def _(ve):
            clk.wait(ve, "dve", "pe", clk.cnt["pe"])
            ve.tensor_copy(bq_cols[:], pmisc[:, 0:24])
            clk.mark(ve.tensor_copy(bofo_col[:], pmisc[:, 24:40]), "dve", 1)
            ve.wait_ge(clk.sem["dve"], clk.cnt["dve"])
            clk.mark(ve.tensor_add(bofo_col[:], bofo_col[:], pmisc[:, 40:56]), "dve", 1)
        self.bq_cols_ready = clk.cnt["dve"]
        self.bofo_ready = clk.cnt["dve"]
        self.free("pmisc", "fo_row", "bo_row", "bq_row")

    def ln_tile(self, ve, se, xt, sq, mneg, rstd, scr1):
        clk = self.clk
        ve.tensor_reduce(scr1[:, 0:1], xt[:], AX.X, ALU.add)
        ve.tensor_mul(sq[:], xt[:], xt[:])
        clk.mark(ve.tensor_reduce(scr1[:, 1:2], sq[:], AX.X, ALU.add), "dve", 1)
        ve.wait_ge(clk.sem["dve"], clk.cnt["dve"])
        ve.tensor_scalar(mneg[:], scr1[:, 0:1], -1.0 / D, None, ALU.mult)
        clk.mark(ve.tensor_scalar(scr1[:, 2:3], scr1[:, 1:2], 1.0 / D, None, ALU.mult), "dve", 1)
        ve.wait_ge(clk.sem["dve"], clk.cnt["dve"])
        clk.mark(ve.tensor_mul(scr1[:, 3:4], mneg[:], mneg[:]), "dve", 1)
        ve.wait_ge(clk.sem["dve"], clk.cnt["dve"])
        clk.mark(ve.tensor_sub(scr1[:, 4:5], scr1[:, 2:3], scr1[:, 3:4]), "dve", 1)

    def p_ln(self):
        clk = self.clk
        self.OT = [self.sb(f"OT{j}", [128, S], BF16) for j in range(8)]
        self.xnT = [self.sb(f"xnT{j}", [128, S], BF16) for j in range(16)]
        sq = self.sb("sq", [128, D], F32)
        xt = [self.sb(f"xt{i}", [128, D], F32) for i in range(2)]
        xnb = [self.sb(f"xnb{i}", [128, D], BF16) for i in range(2)]
        scr1 = self.sb("scr1", [128, 8], F32)
        mneg = self.sb("mneg", [128, 1], F32)
        rstd = self.sb("rstd", [128, 1], F32)
        ptr = [self.ps(f"ptr{i}", [128, 2048], BF16) for i in range(2)]

        xtiles = self.xseq.rearrange("(a p) f -> a p f", p=128)
        dbg_xnT = self.dbg("xnT", [D, S], BF16)
        self.ln_cast_done = {}
        self.ln_evac_done = {}
        self.ln_tr_mark = {}

        for i in range(16):
            bi = i % 2

            @self.on("sp")
            def _(sp, i=i, bi=bi):
                if i >= 2:
                    clk.wait(sp, "sp", "dve", self.ln_cast_done[i - 2])
                clk.mark(sp.dma_start(out=xt[bi][:], in_=xtiles[i]), "dsp", 16)
            dma_x = clk.cnt["dsp"]

            @self.on("dve")
            def _(ve, i=i, bi=bi, dma_x=dma_x):
                clk.wait(ve, "dve", "dsp", dma_x)
                if i == 0:
                    clk.wait(ve, "dve", "pool", self.consts_ready)
                self.ln_tile(ve, None, xt[bi], sq, mneg, rstd, scr1)

            @self.on("act")
            def _(se, i=i):
                clk.wait(se, "act", "dve", clk.cnt["dve"])
                clk.mark(se.activation(rstd[:], scr1[:, 4:5], AF.Sqrt, bias=self.bufs["eps1"][1][:]), "act", 1)

            @self.on("dve")
            def _(ve, i=i, bi=bi):
                clk.wait(ve, "dve", "act", clk.cnt["act"])
                clk.mark(ve.reciprocal(rstd[:], rstd[:]), "dve", 1)
                ve.wait_ge(clk.sem["dve"], clk.cnt["dve"])
                if i >= 2:
                    clk.wait(ve, "dve", "pe", self.ln_tr_mark[i - 2])
                clk.mark(ve.tensor_scalar(xnb[bi][:], xt[bi][:], mneg[:], rstd[:], ALU.add, ALU.mult), "dve", 1)
            self.ln_cast_done[i] = clk.cnt["dve"]

            @self.on("pe")
            def _(te, i=i, bi=bi):
                clk.wait(te, "pe", "dve", self.ln_cast_done[i])
                if i >= 2:
                    clk.wait(te, "pe", "dve", self.ln_evac_done[i - 2])
                last = None
                for j in range(16):
                    last = te.transpose(ptr[bi][:, 128 * j:128 * (j + 1)],
                                        xnb[bi][:, 128 * j:128 * (j + 1)],
                                        self.bufs["ident_b"][1][:])
                clk.mark(last, "pe", 1)
                self.ln_tr_mark[i] = clk.cnt["pe"]

            @self.on("dve")
            def _(ve, i=i, bi=bi):
                clk.wait(ve, "dve", "pe", clk.cnt["pe"])
                last = None
                for j in range(16):
                    last = ve.tensor_copy(self.xnT[j][:, 128 * i:128 * (i + 1)], ptr[bi][:, 128 * j:128 * (j + 1)])
                clk.mark(last, "dve", 1)
            self.ln_evac_done[i] = clk.cnt["dve"]
        self.xnT_done = clk.cnt["dve"]

        if dbg_xnT is not None:
            @self.on("act")
            def _(se):
                clk.wait(se, "act", "dve", self.xnT_done)
                for j in range(16):
                    clk.mark(se.dma_start(out=dbg_xnT[128 * j:128 * (j + 1), :], in_=self.xnT[j][:]), "dact", 16)

        self.free("ptr1", "ptr0")
        self.free("rstd", "mneg", "scr1", "xnb1", "xnb0", "xt1", "xt0", "sq")

    def settle(self, ve):
        return ve.tensor_copy(self.settle_col[0:1, 4:8], self.settle_col[0:1, 0:4])

    def fence(self, te, psum_ap, n):
        return te.matmul(psum_ap, self.bufs["ones1_b"][1][:],
                         self.bufs["zrow_b"][1][0:1, 0:n], start=False, stop=True)

    def barrier(self, tag):
        clk = self.clk
        prods = ("pe", "dve", "act", "pool", "dsp", "dact", "dpool", "cc")
        snap = {p: clk.cnt[p] for p in prods}
        for eng in ("pe", "dve", "act", "pool", "sp"):
            @self.on(eng)
            def _(e, eng=eng):
                for p in prods:
                    if p != eng:
                        clk.wait(e, eng, p, snap[p])

    def transpose_batch(self, tag, jobs, src_wait, extra_pe_waits=()):
        clk = self.clk
        pts = [self.ps(f"ptb_{tag}_{i}", [128, 2048], BF16) for i in range(2)]
        evac_mark = {}
        batches = [jobs[i:i + 16] for i in range(0, len(jobs), 16)]
        for bi, batch in enumerate(batches):
            @self.on("pe")
            def _(te, bi=bi, batch=batch):
                clk.wait(te, "pe", *src_wait)
                for pw in extra_pe_waits:
                    clk.wait(te, "pe", *pw)
                if bi >= 2:
                    clk.wait(te, "pe", "dve", evac_mark[bi - 2])
                last = None
                for k, (src, dst) in enumerate(batch):
                    last = te.transpose(pts[bi % 2][:, 128 * k:128 * (k + 1)], src, self.bufs["ident_b"][1][:])
                clk.mark(last, "pe", 1)
            pe_mark = clk.cnt["pe"]

            @self.on("dve")
            def _(ve, bi=bi, batch=batch, pe_mark=pe_mark):
                clk.wait(ve, "dve", "pe", pe_mark)
                last = None
                for k, (src, dst) in enumerate(batch):
                    last = ve.tensor_copy(dst, pts[bi % 2][:, 128 * k:128 * (k + 1)])
                clk.mark(last, "dve", 1)
            evac_mark[bi] = clk.cnt["dve"]
        self.free(f"ptb_{tag}_1", f"ptb_{tag}_0")
        return clk.cnt["dve"]

    def p_gamma_partials(self):
        if STAGE < 4:
            return
        clk = self.clk
        self.gacc = self.sb("gacc", [128, 28], F32)
        self.grow = self.sb("grow", [1, 12], F32)
        self.sht = [self.sb(f"sht{i}", [128, 2048], F32) for i in range(2)]
        self.tern = self.sb("tern", [128, 2048], BF16)
        self.stg = [self.sb(f"stg{i}", [128, 2048], BF16) for i in range(8)]
        self.shards = [("g", self.gsh, self.agg_in), ("v", self.vsh, self.agv_in), ("o", self.osh, self.ago_in)]
        self.sh_dma = {}
        self.gam_ready = None
        self.ag_mark = {}

    def emit_gamma_slice(self, sh, sl):
        clk = self.clk
        name, ap, _ = self.shards[sh]
        src = ap.rearrange("(b t p) f -> b p t f", t=(8 if sh < 2 else 2), p=128)
        nb = (sh * 8 + sl) % 2

        @self.on("sp")
        def _(sp):
            if sh * 8 + sl < 2:
                clk.wait(sp, "sp", "dve", self.xnT_done)
            else:
                clk.wait(sp, "sp", "dve", self.gam_slice_done[sh * 8 + sl - 2])
            clk.mark(sp.dma_start(out=self.sht[nb][:], in_=src[sl]), "dsp", 16)
        dmam = clk.cnt["dsp"]

        @self.on("dve")
        def _(ve):
            clk.wait(ve, "dve", "dsp", dmam)
            clk.mark(ve.tensor_reduce(self.gacc[:, 8 * sh + sl:8 * sh + sl + 1], self.sht[nb][:],
                                      AX.X, ALU.add, apply_absolute_value=True), "dve", 1)
        if not hasattr(self, "gam_slice_done"):
            self.gam_slice_done = {}
        self.gam_slice_done[sh * 8 + sl] = clk.cnt["dve"]

    def emit_gamma_finishA(self):
        clk = self.clk
        onescol_f = self.bufs["onescol_f"][1]
        pgm = self.ps("pgm", [128, 512], F32)

        @self.on("dve")
        def _(ve):
            ve.wait_ge(clk.sem["dve"], self.gam_slice_done[23])
            ve.tensor_reduce(self.gacc[:, 24:25], self.gacc[:, 0:8], AX.X, ALU.add)
            ve.tensor_reduce(self.gacc[:, 25:26], self.gacc[:, 8:16], AX.X, ALU.add)
            clk.mark(ve.tensor_reduce(self.gacc[:, 26:27], self.gacc[:, 16:24], AX.X, ALU.add), "dve", 1)
        red_mark = clk.cnt["dve"]

        @self.on("pe")
        def _(te):
            clk.wait(te, "pe", "dve", red_mark)
            clk.mark(te.matmul(pgm[0:1, 0:4], onescol_f[:], self.gacc[:, 24:28], start=True, stop=True), "pe", 1)

        @self.on("dve")
        def _(ve):
            clk.wait(ve, "dve", "pe", clk.cnt["pe"])
            clk.mark(ve.tensor_copy(self.grow[0:1, 0:4], pgm[0:1, 0:4]), "dve", 1)
        self.pe_gate_dve = clk.cnt["dve"]

        @self.on("act")
        def _(se):
            clk.wait(se, "act", "dve", clk.cnt["dve"])
            clk.mark(se.dma_start(out=self.gam_in, in_=self.grow[0:1, 0:4]), "dact", 16)
        gin_mark = clk.cnt["dact"]

        @self.on("pool")
        def _(g):
            clk.wait(g, "pool", "dact", gin_mark)
            clk.mark(g.collective_compute("AllReduce", ALU.add, replica_groups=ALL8,
                                          ins=[self.gam_in.opt()], outs=[self.gam_out.opt()]), "cc", 1)
        self.ar_mark = clk.cnt["cc"]
        self.free("pgm")

    def emit_gamma_finishB(self):
        clk = self.clk
        ones1_f = self.bufs["ones1_f"][1]
        gamcols = self.bufs["gamcols"][1]
        pgm2 = self.ps("pgm2", [128, 512], F32)

        @self.on("sp")
        def _(sp):
            clk.wait(sp, "sp", "cc", self.ar_mark)
            clk.mark(sp.dma_start(out=self.grow[0:1, 0:4], in_=self.gam_out), "dsp", 16)
        gout_mark = clk.cnt["dsp"]

        @self.on("dve")
        def _(ve):
            clk.wait(ve, "dve", "dsp", gout_mark)
            clk.mark(ve.tensor_scalar(self.grow[0:1, 0:4], self.grow[0:1, 0:4],
                                      1.0 / (H * D), 1e-5, ALU.mult, ALU.max), "dve", 1)
            ve.wait_ge(clk.sem["dve"], clk.cnt["dve"])
            if GNUDGE:
                clk.mark(ve.tensor_scalar(self.grow[0:1, 4:8], self.grow[0:1, 0:4],
                                          GNUDGE, None, ALU.mult), "dve", 1)
                ve.wait_ge(clk.sem["dve"], clk.cnt["dve"])
                clk.mark(ve.tensor_add(self.grow[0:1, 0:4], self.grow[0:1, 0:4],
                                       self.grow[0:1, 4:8]), "dve", 1)
                ve.wait_ge(clk.sem["dve"], clk.cnt["dve"])
            ve.tensor_scalar(self.grow[0:1, 4:8], self.grow[0:1, 0:4], 1.0 / 127.0, None, ALU.mult)
            clk.mark(ve.reciprocal(self.grow[0:1, 8:12], self.grow[0:1, 0:4]), "dve", 1)
        gr_mark = clk.cnt["dve"]

        @self.on("pe")
        def _(te):
            clk.wait(te, "pe", "dve", gr_mark)
            clk.wait(te, "pe", "dve", self.pe_gate_dve)
            clk.mark(te.matmul(pgm2[:, 0:8], ones1_f[:], self.grow[0:1, 4:12], start=True, stop=True), "pe", 1)

        @self.on("dve")
        def _(ve):
            clk.wait(ve, "dve", "pe", clk.cnt["pe"])
            clk.mark(ve.tensor_copy(gamcols[:], pgm2[:, 0:8]), "dve", 1)
        self.gam_ready = clk.cnt["dve"]
        self.pe_gate_dve = clk.cnt["dve"]
        self.free("pgm2")
        dbg_gam = self.dbg("gam", [1, 4])
        if dbg_gam is not None:
            @self.on("act")
            def _(se):
                clk.wait(se, "act", "dve", self.gam_ready)
                clk.mark(se.dma_start(out=dbg_gam, in_=self.grow[0:1, 0:4]), "dact", 16)

    def emit_quant_slice(self, sh, sl):
        clk = self.clk
        name, ap, _ = self.shards[sh]
        gamcols = self.bufs["gamcols"][1]
        tpr = 8 if sh < 2 else 2
        src = ap.rearrange("(b t p) f -> b p t f", t=tpr, p=128)
        nb = sl % 2
        if not hasattr(self, "q_used"):
            self.q_used = {}
            self.q_tern_free = 0

        @self.on("sp")
        def _(sp):
            clk.wait(sp, "sp", "dve", self.gam_slice_done[23])
            prev = self.q_used.get((sh, sl - 1)) or self.q_used.get((sh - 1, 7))
            if prev:
                clk.wait(sp, "sp", "dve", prev)
            clk.mark(sp.dma_start(out=self.sht[nb][:], in_=src[sl]), "dsp", 16)
        dmam = clk.cnt["dsp"]

        @self.on("dve")
        def _(ve):
            clk.wait(ve, "dve", "dsp", dmam)
            ve.wait_ge(clk.sem["dve"], self.gam_ready)
            clk.wait(ve, "dve", "pe", self.q_tern_free)
            clk.mark(ve.tensor_scalar(self.sht[nb][:], self.sht[nb][:], gamcols[:, 4 + sh:5 + sh],
                                      MAGIC, ALU.mult, ALU.add), "dve", 1)
            ve.wait_ge(clk.sem["dve"], clk.cnt["dve"])
            clk.mark(ve.tensor_scalar(self.sht[nb][:], self.sht[nb][:], MAGIC - 1.0, MAGIC + 1.0,
                                      ALU.max, ALU.min), "dve", 1)
            ve.wait_ge(clk.sem["dve"], clk.cnt["dve"])
            clk.mark(ve.tensor_scalar(self.tern[:], self.sht[nb][:], -MAGIC, None, ALU.add), "dve", 1)
        self.q_used[(sh, sl)] = clk.cnt["dve"]
        tern_mark = clk.cnt["dve"]

        jobs = []
        if sh < 2:
            for t in range(8):
                for c in range(2):
                    srcap = self.tern[:, 256 * t + 128 * c:256 * t + 128 * (c + 1)]
                    col = 1024 * sl + 128 * t
                    dst = self.stg[4 * c + col // 2048][:, col % 2048:col % 2048 + 128]
                    jobs.append((srcap, dst))
        else:
            for t in range(2):
                for c in range(8):
                    srcap = self.tern[:, 1024 * t + 128 * c:1024 * t + 128 * (c + 1)]
                    dst = self.stg[c][:, 256 * sl + 128 * t:256 * sl + 128 * (t + 1)]
                    jobs.append((srcap, dst))
        tm = self.transpose_batch(f"qs{sh}_{sl}", jobs, ("dve", tern_mark),
                                  extra_pe_waits=((("dact", self.stg_free.get(sh, 0)),)
                                                  if hasattr(self, "stg_free") else ()))
        self.q_tern_free = clk.cnt["pe"]
        self.pe_gate_dve = max(getattr(self, "pe_gate_dve", 0), tm)
        return tm

    def emit_shard_out(self, sh):
        clk = self.clk
        name, _, ag_in = self.shards[sh]
        ag_out = {"g": self.agg_out, "v": self.agv_out, "o": self.ago_out}[name]
        @self.on("dve")
        def _(ve):
            ve.wait_ge(clk.sem["dve"], clk.cnt["dve"])
            clk.mark(self.settle(ve), "dve", 1)
        qm = self.clk.cnt["dve"]

        @self.on("act")
        def _(se):
            clk.wait(se, "act", "dve", qm)
            if sh < 2:
                for c in range(2):
                    for s4 in range(4):
                        clk.mark(se.dma_start(out=ag_in[128 * c:128 * (c + 1), 2048 * s4:2048 * (s4 + 1)],
                                              in_=self.stg[4 * c + s4][:]), "dact", 16)
            else:
                for c in range(8):
                    clk.mark(se.dma_start(out=ag_in[128 * c:128 * (c + 1), :], in_=self.stg[c][:]), "dact", 16)
        stgd = clk.cnt["dact"]

        @self.on("pool")
        def _(g):
            clk.wait(g, "pool", "dact", stgd)
            clk.mark(g.collective_compute("AllGather", ALU.bypass, replica_groups=ALL8,
                                          ins=[ag_in.opt()], outs=[ag_out.opt()]), "cc", 1)
        if not hasattr(self, "stg_free"):
            self.stg_free = {}
        self.stg_free[sh + 1] = clk.cnt["dact"]
        self.ag_mark[name] = clk.cnt["cc"]

        dbg = self.dbg("tern" + name, [256 if sh < 2 else 1024, H if sh < 2 else D], BF16)
        if dbg is not None:
            @self.on("act")
            def _(se):
                clk.wait(se, "act", "cc", clk.cnt["cc"])
                clk.mark(se.dma_start(out=dbg, in_=ag_in), "dact", 16)

    def p_heads(self):
        clk = self.clk
        ident_b = self.bufs["ident_b"][1]
        diag = self.bufs["diag"][1]
        ones1_f = self.bufs["ones1_f"][1]
        onescol_b = self.bufs["onescol_b"][1]
        bq_cols = self.bufs["bq_cols"][1]

        whb = [self.sb(f"whb{i}", [128, 16 * 384], BF16) for i in range(2)]
        QTh = self.sb("QTh", [128, S], BF16)
        KTh = self.sb("KTh", [128, S], BF16)
        VTh = self.sb("VTh", [128, S], BF16)
        vtm = self.sb("vtm", [128, S], BF16)
        sqb = self.sb("sqb", [128, S], BF16)
        Pb = [self.sb(f"Pb{i}", [128, 1024], BF16) for i in range(2)]
        binv = self.sb("binv", [128, 1024], F32)
        linv = self.sb("linv", [1, 1024], F32)
        scrC = self.sb("scrC", [1, 16], F32)
        nC = self.sb("nC", [128, 1], F32)

        dbg_QT = self.dbg("QT", [1024, S], BF16)
        dbg_KT = self.dbg("KT", [1024, S], BF16)
        dbg_OT = self.dbg("OT", [1024, S], BF16)

        wtiles = self.wqkv.rearrange("(a p) f -> a p f", p=128)
        whb_dma = [0, 0]
        evac_done = {}
        prev_ot_mark = 0

        nheads = NHEADS if STAGE >= 1 else 0
        for h in range(nheads):
            hb = h % 2

            @self.on("pool")
            def _(g, h=h, hb=hb):
                if h < 2:
                    clk.wait(g, "pool", "dve", self.xnT_done)
                if h >= 2:
                    clk.wait(g, "pool", "dve", evac_done[h - 2])
                last = None
                for j in range(16):
                    last = g.dma_start(out=whb[hb][:, 384 * j:384 * (j + 1)],
                                       in_=wtiles[j][:, 384 * h:384 * (h + 1)])
                    clk.mark(last, "dpool", 16)
            whb_dma[hb] = clk.cnt["dpool"]

            pq = [self.ps(f"pq{h}_{i}", [128, 512], F32) for i in range(2)]
            qkv_mark = {}
            for e in range(3):
                for tb in range(4):
                    idx = e * 4 + tb

                    @self.on("pe")
                    def _(te, h=h, hb=hb, e=e, tb=tb, idx=idx):
                        clk.wait(te, "pe", "dpool", whb_dma[hb])
                        clk.wait(te, "pe", "dve", self.xnT_done)
                        if idx < 2:
                            clk.wait(te, "pe", "dve", prev_ot_mark)
                            clk.wait(te, "pe", "dve", getattr(self, "pe_gate_dve", 0))
                        if idx >= 2:
                            clk.wait(te, "pe", "dve", qkv_mark[idx - 2])
                        last = None
                        for j in range(16):
                            last = te.matmul(pq[idx % 2][:], whb[hb][:, 384 * j + 128 * e:384 * j + 128 * (e + 1)],
                                             self.xnT[j][:, 512 * tb:512 * (tb + 1)],
                                             start=(j == 0), stop=(j == 15))
                        clk.mark(last, "pe", 1)
                    pe_mark = clk.cnt["pe"]

                    @self.on("dve")
                    def _(ve, h=h, e=e, tb=tb, idx=idx, pe_mark=pe_mark):
                        clk.wait(ve, "dve", "pe", pe_mark)
                        if idx == 0 and dbg_QT is not None:
                            clk.wait(ve, "dve", "dact", clk.cnt["dact"])
                        dst = (QTh, KTh, VTh)[e]
                        bcol = bq_cols[:, 3 * h + e:3 * h + e + 1]
                        if e == 0:
                            inst = ve.tensor_scalar(dst[:, 512 * tb:512 * (tb + 1)], pq[idx % 2][:],
                                                    bcol, INV_SQRT_HD, ALU.add, ALU.mult)
                        else:
                            inst = ve.tensor_scalar(dst[:, 512 * tb:512 * (tb + 1)], pq[idx % 2][:],
                                                    bcol, None, ALU.add)
                        clk.mark(inst, "dve", 1)
                    qkv_mark[idx] = clk.cnt["dve"]
            qkv_done = clk.cnt["dve"]
            evac_done[h] = qkv_done
            self.free(f"pq{h}_1", f"pq{h}_0")
            if STAGE < 2:
                prev_ot_mark = clk.cnt["dve"]
                continue

            ptv = self.ps(f"ptv{h}", [128, 2048], BF16)

            @self.on("pe")
            def _(te, h=h, qkv_done=qkv_done):
                clk.wait(te, "pe", "dve", qkv_done)
                last = None
                for c in range(16):
                    last = te.transpose(ptv[:, 128 * c:128 * (c + 1)], VTh[:, 128 * c:128 * (c + 1)], ident_b[:])
                clk.mark(last, "pe", 1)
            ptv_mark = clk.cnt["pe"]

            @self.on("dve")
            def _(ve, ptv_mark=ptv_mark):
                clk.wait(ve, "dve", "pe", ptv_mark)
                clk.mark(ve.tensor_copy(vtm[:], ptv[:]), "dve", 1)
            vtm_done = clk.cnt["dve"]
            self.free(f"ptv{h}")

            pC = self.ps(f"pC{h}", [1, 512], F32)
            pB = self.ps(f"pB{h}", [128, 1], F32)
            for which, src in ((0, QTh), (1, KTh)):
                @self.on("act")
                def _(se, which=which, src=src, qkv_done=qkv_done):
                    clk.wait(se, "act", "dve", qkv_done)
                    if which == 1:
                        clk.wait(se, "act", "pe", clk.cnt["pe"])
                    clk.mark(se.activation(sqb[:], src[:], AF.Square), "act", 1)
                sq_mark = clk.cnt["act"]
                for s4 in range(4):
                    @self.on("pe")
                    def _(te, which=which, s4=s4, sq_mark=sq_mark, vtm_done=vtm_done):
                        clk.wait(te, "pe", "act", sq_mark)
                        clk.wait(te, "pe", "dve", vtm_done)
                        if which + s4 > 0:
                            clk.wait(te, "pe", "dve", clk.cnt["dve"])
                        clk.mark(te.matmul(pC[:], onescol_b[:], sqb[:, 512 * s4:512 * (s4 + 1)],
                                           start=True, stop=True), "pe", 1)

                    @self.on("dve")
                    def _(ve, which=which, s4=s4):
                        clk.wait(ve, "dve", "pe", clk.cnt["pe"])
                        clk.mark(ve.tensor_reduce(scrC[0:1, 4 * which + s4:4 * which + s4 + 1],
                                                  pC[0:1, :], AX.X, ALU.max), "dve", 1)

            @self.on("dve")
            def _(ve):
                ve.wait_ge(clk.sem["dve"], clk.cnt["dve"])
                ve.tensor_reduce(scrC[0:1, 8:9], scrC[0:1, 0:4], AX.X, ALU.max)
                clk.mark(ve.tensor_reduce(scrC[0:1, 9:10], scrC[0:1, 4:8], AX.X, ALU.max), "dve", 1)
                ve.wait_ge(clk.sem["dve"], clk.cnt["dve"])
                clk.mark(ve.tensor_mul(scrC[0:1, 10:11], scrC[0:1, 8:9], scrC[0:1, 9:10]), "dve", 1)

            @self.on("act")
            def _(se):
                clk.wait(se, "act", "dve", clk.cnt["dve"])
                clk.mark(se.activation(scrC[0:1, 11:12], scrC[0:1, 10:11], AF.Sqrt), "act", 1)

            @self.on("dve")
            def _(ve):
                clk.wait(ve, "dve", "act", clk.cnt["act"])
                clk.mark(ve.tensor_scalar(scrC[0:1, 12:13], scrC[0:1, 11:12], -1.0, None, ALU.mult), "dve", 1)
            negc_mark = clk.cnt["dve"]

            @self.on("pe")
            def _(te, negc_mark=negc_mark):
                clk.wait(te, "pe", "dve", negc_mark)
                clk.mark(te.matmul(pB[:], ones1_f[:], scrC[0:1, 12:13], start=True, stop=True), "pe", 1)

            @self.on("dve")
            def _(ve):
                clk.wait(ve, "dve", "pe", clk.cnt["pe"])
                clk.mark(ve.tensor_copy(nC[:], pB[:]), "dve", 1)
            nc_done = clk.cnt["dve"]
            self.free(f"pB{h}", f"pC{h}")
            if STAGE < 3:
                prev_ot_mark = clk.cnt["dve"]
                continue

            pS = [self.ps(f"pS{h}_{i}", [128, 1024], F32) for i in range(2)]
            pAV = self.ps(f"pAV{h}", [128, 1024], F32)
            pL = self.ps(f"pL{h}", [1, 1024], F32)
            prev_exp_last = 0
            prev_qh_ot = 0
            for qh in range(2):
                climit = 8 + 8 * qh
                exp_mark = {}
                av_mark = {}
                for c in range(climit):
                    cb = c % 2
                    qoff = max(0, 128 * c - 1024 * qh)
                    pieces = []
                    if qoff < 512:
                        pieces.append((qoff, 512))
                    pieces.append((max(qoff, 512), 1024))

                    @self.on("pe")
                    def _(te, h=h, qh=qh, c=c, cb=cb, pieces=pieces, vtm_done=vtm_done):
                        clk.wait(te, "pe", "dve", vtm_done)
                        if c < 2:
                            clk.wait(te, "pe", "dve", nc_done)
                            clk.wait(te, "pe", "act", prev_exp_last)
                            clk.wait(te, "pe", "dve", prev_qh_ot)
                        if c >= 2:
                            clk.wait(te, "pe", "act", exp_mark[c - 2])
                        for (a, b) in pieces:
                            clk.mark(te.matmul(pS[cb][:, a:b], KTh[:, 128 * c:128 * (c + 1)],
                                               QTh[:, 1024 * qh + a:1024 * qh + b], start=True, stop=True), "pe", 1)
                    s_mark = clk.cnt["pe"]

                    @self.on("act")
                    def _(se, c=c, cb=cb, qoff=qoff, s_mark=s_mark, nc_done=nc_done):
                        clk.wait(se, "act", "dve", nc_done)
                        clk.wait(se, "act", "pe", s_mark)
                        if c >= 2:
                            clk.wait(se, "act", "pe", av_mark[c - 2])
                        clk.mark(se.activation(Pb[cb][:, qoff:1024], pS[cb][:, qoff:1024], AF.Exp,
                                               bias=nC[:]), "act", 1)
                    exp_mark[c] = clk.cnt["act"]

                    dcol = 128 * c - 1024 * qh
                    masked = 0 <= dcol < 1024
                    if masked:
                        @self.on("dve")
                        def _(ve, cb=cb, dcol=dcol):
                            clk.wait(ve, "dve", "act", exp_mark[c])
                            clk.mark(ve.tensor_mul(Pb[cb][:, dcol:dcol + 128], Pb[cb][:, dcol:dcol + 128],
                                                   diag[:]), "dve", 1)
                        mask_mark = clk.cnt["dve"]
                    else:
                        mask_mark = None

                    @self.on("pe")
                    def _(te, h=h, qh=qh, c=c, cb=cb, pieces=pieces, climit=climit,
                          mask_mark=mask_mark):
                        clk.wait(te, "pe", "act", exp_mark[c])
                        if c == 0:
                            clk.wait(te, "pe", "dve", prev_qh_ot)
                        if mask_mark is not None:
                            clk.wait(te, "pe", "dve", mask_mark)
                        for (a, b) in pieces:
                            first = (c == 0)
                            lastc = (c == climit - 1) or (a == 0 and 128 * (c + 1) - 1024 * qh >= 512)
                            te.matmul(pAV[:, a:b], vtm[:, 128 * c:128 * (c + 1)], Pb[cb][:, a:b],
                                      start=first, stop=lastc)
                            last = te.matmul(pL[0:1, a:b], onescol_b[:], Pb[cb][:, a:b],
                                             start=first, stop=lastc)
                        clk.mark(last, "pe", 1)
                    av_mark[c] = clk.cnt["pe"]

                @self.on("dve")
                def _(ve, qh=qh):
                    clk.wait(ve, "dve", "pe", clk.cnt["pe"])
                    clk.mark(ve.reciprocal(linv[:], pL[0:1, :]), "dve", 1)
                linv_mark = clk.cnt["dve"]

                @self.on("pe")
                def _(te, linv_mark=linv_mark):
                    clk.wait(te, "pe", "dve", linv_mark)
                    te.matmul(pS[0][:, 0:512], ones1_f[:], linv[0:1, 0:512], start=True, stop=True)
                    clk.mark(te.matmul(pS[0][:, 512:1024], ones1_f[:], linv[0:1, 512:1024],
                                       start=True, stop=True), "pe", 1)

                @self.on("dve")
                def _(ve, h=h, qh=qh):
                    clk.wait(ve, "dve", "pe", clk.cnt["pe"])
                    clk.mark(ve.tensor_copy(binv[:], pS[0][:]), "dve", 1)
                    ve.wait_ge(clk.sem["dve"], clk.cnt["dve"])
                    clk.mark(ve.tensor_mul(self.OT[h][:, 1024 * qh:1024 * (qh + 1)], pAV[:], binv[:]), "dve", 1)
                prev_exp_last = exp_mark[climit - 1]
                prev_qh_ot = clk.cnt["dve"]
            prev_ot_mark = clk.cnt["dve"]
            self.free(f"pL{h}", f"pAV{h}", f"pS{h}_1", f"pS{h}_0")

            if dbg_QT is not None:
                @self.on("act")
                def _(se, h=h, qkv_done=qkv_done):
                    clk.wait(se, "act", "dve", qkv_done)
                    clk.mark(se.dma_start(out=dbg_QT[128 * h:128 * (h + 1), :], in_=QTh[:]), "dact", 16)
                    clk.mark(se.dma_start(out=dbg_KT[128 * h:128 * (h + 1), :], in_=KTh[:]), "dact", 16)

            if STAGE >= 4:
                self.head_extras(h)

        if STAGE >= 4 and SUBQ and not QINTER:
            self.quant_tail()
        self.OT_done = clk.cnt["dve"]
        if dbg_OT is not None:
            @self.on("act")
            def _(se):
                clk.wait(se, "act", "dve", self.OT_done)
                for h in range(8):
                    clk.mark(se.dma_start(out=dbg_OT[128 * h:128 * (h + 1), :], in_=self.OT[h][:]), "dact", 16)

        self.free("nC", "scrC", "linv", "binv", "Pb1", "Pb0", "sqb", "vtm", "VTh", "KTh", "QTh", "whb1", "whb0")
        if STAGE >= 4:
            self.free("stg7", "stg6", "stg5", "stg4", "stg3", "stg2", "stg1", "stg0",
                      "tern", "sht1", "sht0", "grow", "gacc")
        for j in reversed(range(16)):
            self.free(f"xnT{j}")

    def head_extras(self, h):
        if h == 0:
            for i in range(12):
                self.emit_gamma_slice(i // 8, i % 8)
        elif h == 1:
            for i in range(12, 24):
                self.emit_gamma_slice(i // 8, i % 8)
            if SUBF:
                self.emit_gamma_finishA()
        elif h == 2:
            if SUBF:
                self.emit_gamma_finishB()
        elif SUBQ and QINTER:
            if h == 3:
                for sl in range(8):
                    self.emit_quant_slice(0, sl)
            elif h == 4:
                self.emit_shard_out(0)
                for sl in range(4):
                    self.emit_quant_slice(1, sl)
            elif h == 5:
                for sl in range(4, 8):
                    self.emit_quant_slice(1, sl)
                self.emit_shard_out(1)
            elif h == 6:
                for sl in range(8):
                    self.emit_quant_slice(2, sl)
            elif h == 7:
                self.emit_shard_out(2)

    def quant_tail(self):
        for sh in range(3):
            for sl in range(8):
                self.emit_quant_slice(sh, sl)
            self.emit_shard_out(sh)

    def p_shard_quant(self):
        pass

    def p_woutqT(self):
        if STAGE < 5:
            return
        clk = self.clk
        if STAGE >= 6:
            self.qT = [self.sb(f"qT{j}", [128, 1024], BF16) for j in range(16)]
            self.scg = self.sb("scg", [128, 8], F32)
            self.scv = self.sb("scv", [128, 8], F32)
            self._xq = [self.sb(f"xq{i}", [128, D], F32) for i in range(2)]
            self._xsq = self.sb("xsq", [128, D], F32)
            self._xnf = self.sb("xnf", [128, D], F32)
            self._q8 = self.sb("q8", [128, D], BF16)
            self._scr2 = self.sb("scr2", [128, 8], F32)
            self._mn2 = self.sb("mn2", [128, 1], F32)
            self._rs2 = self.sb("rs2", [128, 1], F32)
            self._cast_mark = {}
        woutb = [self.sb(f"woutb{d}", [128, D], BF16) for d in range(8)]
        rss = [self.sb(f"rss{i}", [128, 512], F32) for i in range(2)]
        prs = [self.ps(f"prs{i}", [128, 512], F32) for i in range(2)]
        wtl = self.wout.rearrange("(a p) f -> a p f", p=128)

        @self.on("pool")
        def _(g):
            clk.wait(g, "pool", "dve", clk.cnt["dve"])
            clk.wait(g, "pool", "pe", clk.cnt["pe"])
            for d in range(8):
                clk.mark(g.dma_start(out=woutb[d][:], in_=wtl[d]), "dpool", 16)
        wload = clk.cnt["dpool"]

        ev_mark = {}
        st_mark = {}
        for tb in range(4):
            for dc in range(16):
                idx = tb * 16 + dc
                if STAGE >= 6 and idx % 8 == 0:
                    self.emit_qT_tile(idx // 8, part=0)
                if STAGE >= 6 and idx % 8 == 6:
                    self.emit_qT_tile(idx // 8, part=1)

                @self.on("pe")
                def _(te, tb=tb, dc=dc, idx=idx):
                    clk.wait(te, "pe", "dpool", wload)
                    clk.wait(te, "pe", "dve", self.OT_done)
                    if idx < 2:
                        clk.wait(te, "pe", "dve", getattr(self, "pe_gate_dve", 0))
                    if idx >= 2:
                        clk.wait(te, "pe", "dve", ev_mark[idx - 2])
                    last = None
                    for d in range(8):
                        last = te.matmul(prs[idx % 2][:], woutb[d][:, 128 * dc:128 * (dc + 1)],
                                         self.OT[d][:, 512 * tb:512 * (tb + 1)],
                                         start=(d == 0), stop=(d == 7))
                    clk.mark(last, "pe", 1)
                mm = clk.cnt["pe"]

                @self.on("dve")
                def _(ve, idx=idx, mm=mm):
                    clk.wait(ve, "dve", "pe", mm)
                    if idx >= 2:
                        clk.wait(ve, "dve", "dact", st_mark[idx - 2])
                    clk.mark(ve.tensor_copy(rss[idx % 2][:], prs[idx % 2][:]), "dve", 1)
                    ve.wait_ge(clk.sem["dve"], clk.cnt["dve"])
                    clk.mark(self.settle(ve), "dve", 1)
                ev_mark[idx] = clk.cnt["dve"]

                @self.on("act")
                def _(se, tb=tb, dc=dc, idx=idx):
                    clk.wait(se, "act", "dve", ev_mark[idx])
                    j = tb // 2
                    clk.mark(se.dma_start(
                        out=self.rs_in[2048 * j + 128 * dc:2048 * j + 128 * (dc + 1),
                                       512 * (tb % 2):512 * (tb % 2) + 512],
                        in_=rss[idx % 2][:]), "dact", 16)
                st_mark[idx] = clk.cnt["dact"]

        @self.on("pool")
        def _(g):
            clk.wait(g, "pool", "dact", clk.cnt["dact"])
            clk.mark(g.collective_compute("ReduceScatter", ALU.add, replica_groups=PAIRS,
                                          ins=[self.rs_in.opt()], outs=[self.rs_out.opt()]), "cc", 1)
        self.rs_mark = clk.cnt["cc"]
        if STAGE >= 6:
            self.emit_qT_finish()
        self.free("prs1", "prs0")
        self.free("rss1", "rss0")
        self.free(*[f"woutb{d}" for d in reversed(range(8))])
        if STAGE >= 6:
            self.free("rs2", "mn2", "scr2", "q8", "xnf", "xsq", "xq1", "xq0")
        else:
            self.free(*[f"OT{j}" for j in reversed(range(8))])

        dbg = self.dbg("attn", [D, HT])
        if dbg is not None:
            @self.on("act")
            def _(se):
                clk.wait(se, "act", "cc", self.rs_mark)
                clk.mark(se.dma_start(out=dbg, in_=self.rs_out), "dact", 16)
        dbg2 = self.dbg("rsin", [2 * D, HT])
        if dbg2 is not None:
            @self.on("act")
            def _(se):
                clk.wait(se, "act", "cc", self.rs_mark)
                clk.mark(se.dma_start(out=dbg2, in_=self.rs_in), "dact", 16)

    def emit_qT_tile(self, i, part):
        clk = self.clk
        gxcol = self.bufs["gxcol"][1]
        xq, xsq, xnf, q8 = self._xq, self._xsq, self._xnf, self._q8
        scr2, mn2, rs2 = self._scr2, self._mn2, self._rs2
        bi = i % 2
        if part == 1:
            jobs = [(q8[:, 128 * j:128 * (j + 1)], self.qT[j][:, 128 * i:128 * (i + 1)])
                    for j in range(16)]
            self.transpose_batch(f"qt{i}", jobs, ("dve", self._cast_mark[i]))
            self.q8_free = clk.cnt["pe"]
            return

        xht = self.xhalf.rearrange("(a p) f -> a p f", p=128)

        @self.on("sp")
        def _(sp, i=i, bi=bi):
            if i < 2:
                clk.wait(sp, "sp", "dve", clk.cnt["dve"])
                clk.wait(sp, "sp", "pe", clk.cnt["pe"])
            else:
                clk.wait(sp, "sp", "dve", self._cast_mark[i - 2])
            clk.mark(sp.dma_start(out=xq[bi][:], in_=xht[i]), "dsp", 16)
        dmam = clk.cnt["dsp"]

        @self.on("dve")
        def _(ve, i=i, bi=bi, dmam=dmam):
            clk.wait(ve, "dve", "dsp", dmam)
            self.ln_tile(ve, None, xq[bi], xsq, mn2, rs2, scr2)

        @self.on("act")
        def _(se):
            clk.wait(se, "act", "dve", clk.cnt["dve"])
            clk.mark(se.activation(rs2[:], scr2[:, 4:5], AF.Sqrt, bias=self.bufs["eps1"][1][:]), "act", 1)

        @self.on("dve")
        def _(ve, i=i, bi=bi):
            clk.wait(ve, "dve", "act", clk.cnt["act"])
            clk.mark(ve.reciprocal(rs2[:], rs2[:]), "dve", 1)
            ve.wait_ge(clk.sem["dve"], clk.cnt["dve"])
            clk.mark(ve.tensor_scalar(xnf[:], xq[bi][:], mn2[:], rs2[:], ALU.add, ALU.mult), "dve", 1)
            ve.wait_ge(clk.sem["dve"], clk.cnt["dve"])
            clk.mark(ve.tensor_reduce(gxcol[:, i:i + 1], xnf[:], AX.X, ALU.max,
                                      apply_absolute_value=True), "dve", 1)
            ve.wait_ge(clk.sem["dve"], clk.cnt["dve"])
            clk.mark(ve.tensor_scalar(gxcol[:, i:i + 1], gxcol[:, i:i + 1], 1e-5, None, ALU.max), "dve", 1)
            ve.wait_ge(clk.sem["dve"], clk.cnt["dve"])
            clk.mark(ve.reciprocal(scr2[:, 6:7], gxcol[:, i:i + 1]), "dve", 1)
            ve.wait_ge(clk.sem["dve"], clk.cnt["dve"])
            clk.mark(ve.tensor_scalar(scr2[:, 7:8], scr2[:, 6:7], 127.0, None, ALU.mult), "dve", 1)
            ve.wait_ge(clk.sem["dve"], clk.cnt["dve"])
            clk.mark(ve.tensor_scalar(xnf[:], xnf[:], scr2[:, 7:8], MAGIC, ALU.mult, ALU.add), "dve", 1)
            ve.wait_ge(clk.sem["dve"], clk.cnt["dve"])
            clk.wait(ve, "dve", "pe", getattr(self, "q8_free", 0))
            clk.mark(ve.tensor_scalar(q8[:], xnf[:], -MAGIC, None, ALU.add), "dve", 1)
        self._cast_mark[i] = clk.cnt["dve"]

    def emit_qT_finish(self):
        clk = self.clk
        gxcol = self.bufs["gxcol"][1]
        gamcols = self.bufs["gamcols"][1]

        @self.on("dve")
        def _(ve):
            ve.wait_ge(clk.sem["dve"], clk.cnt["dve"])
            ve.tensor_scalar(self.scg[:], gxcol[:], gamcols[:, 0:1], None, ALU.mult)
            clk.mark(ve.tensor_scalar(self.scv[:], gxcol[:], gamcols[:, 1:2], None, ALU.mult), "dve", 1)
        self.qT_done = clk.cnt["dve"]

    def p_gateval(self):
        if STAGE < 7:
            return
        clk = self.clk
        gzcol = self.bufs["gzcol"][1]
        ones1_b = self.bufs["ones1_b"][1]
        wgg = [self.sb(f"wgg{i}", [128, H], BF16) for i in range(2)]
        wgv = [self.sb(f"wgv{i}", [128, H], BF16) for i in range(2)]
        gbrow = self.sb("gbrow", [1, H], BF16)
        vbrow = self.sb("vbrow", [1, H], BF16)
        gbb = self.sb("gbb", [128, 512], F32)
        vbb = self.sb("vbb", [128, 512], F32)
        gpre = self.sb("gpre", [128, 512], F32)
        gsil = self.sb("gsil", [128, 512], F32)
        vpre = self.sb("vpre", [128, 512], F32)
        zsl = [self.sb(f"zsl{i}", [128, 512], F32) for i in range(2)]
        zred = self.sb("zred", [128, 1], F32)
        pz = [self.ps(f"pz{i}", [128, 512], F32) for i in range(4)]
        pb2 = self.ps("pb2", [128, 1024], F32)
        agg_t = self.agg_out.rearrange("(a p) f -> a p f", p=128)
        agv_t = self.agv_out.rearrange("(a p) f -> a p f", p=128)

        @self.on("pool")
        def _(g):
            clk.wait(g, "pool", "dve", clk.cnt["dve"])
            clk.wait(g, "pool", "pe", clk.cnt["pe"])
            g.memset(gzcol[:], 0.0)
            clk.mark(g.dma_start(out=gbrow[:], in_=self.gbi), "dpool", 16)
            clk.mark(g.dma_start(out=vbrow[:], in_=self.vbi), "dpool", 16)
        brows = clk.cnt["dpool"]
        gz_init = clk.cnt["pool"]

        wload = {}
        chain_mark = {}
        evac_mark = {}
        zdma_mark = {}
        for g_ in range(16):
            gb_ = g_ % 2

            @self.on("sp")
            def _(sp, g_=g_, gb_=gb_):
                clk.wait(sp, "sp", "cc", self.ag_mark["v"])
                if g_ < 2:
                    clk.wait(sp, "sp", "dve", clk.cnt["dve"])
                else:
                    clk.wait(sp, "sp", "pe", chain_mark[(g_ - 2, 7, "v")])
                last = None
                for d in range(16):
                    last = sp.dma_start(out=wgg[gb_][:, 512 * d:512 * (d + 1)],
                                        in_=agg_t[d][:, 512 * g_:512 * (g_ + 1)])
                    clk.mark(last, "dsp", 16)
                for d in range(16):
                    last = sp.dma_start(out=wgv[gb_][:, 512 * d:512 * (d + 1)],
                                        in_=agv_t[d][:, 512 * g_:512 * (g_ + 1)])
                    clk.mark(last, "dsp", 16)
            wload[g_] = clk.cnt["dsp"]

            @self.on("pe")
            def _(te, g_=g_):
                clk.wait(te, "pe", "dpool", brows)
                if g_ >= 1:
                    clk.wait(te, "pe", "dve", evac_mark[(g_ - 1, 0, "bias")])
                te.matmul(pb2[:, 0:512], ones1_b[:], gbrow[0:1, 512 * g_:512 * (g_ + 1)], start=True, stop=True)
                clk.mark(te.matmul(pb2[:, 512:1024], ones1_b[:], vbrow[0:1, 512 * g_:512 * (g_ + 1)],
                                   start=True, stop=True), "pe", 1)
            bias_pe = clk.cnt["pe"]

            @self.on("dve")
            def _(ve, g_=g_, bias_pe=bias_pe):
                clk.wait(ve, "dve", "pe", bias_pe)
                ve.tensor_copy(gbb[:], pb2[:, 0:512])
                clk.mark(ve.tensor_copy(vbb[:], pb2[:, 512:1024]), "dve", 1)
            evac_mark[(g_, 0, "bias")] = clk.cnt["dve"]
            bias_dve = clk.cnt["dve"]

            for tc in range(8):
                ci = 2 * (8 * g_ + tc)
                for which in ("g", "v"):
                    pzi = (ci + (0 if which == "g" else 1)) % 4

                    @self.on("pe")
                    def _(te, g_=g_, gb_=gb_, tc=tc, which=which, pzi=pzi):
                        clk.wait(te, "pe", "dsp", wload[g_])
                        clk.wait(te, "pe", "dve", self.qT_done)
                        prev = evac_mark.get((g_, tc - 2, which)) or evac_mark.get((g_ - 1, 6 + tc, which))
                        if prev:
                            clk.wait(te, "pe", "dve", prev)
                        wsrc = wgg[gb_] if which == "g" else wgv[gb_]
                        last = None
                        for d in range(16):
                            last = te.matmul(pz[pzi][:], self.qT[d][:, 128 * tc:128 * (tc + 1)],
                                             wsrc[:, 512 * d:512 * (d + 1)],
                                             start=(d == 0), stop=(d == 15))
                        clk.mark(last, "pe", 1)
                    chain_mark[(g_, tc, which)] = clk.cnt["pe"]

                    if which == "g":
                        @self.on("dve")
                        def _(ve, g_=g_, tc=tc, pzi=pzi):
                            clk.wait(ve, "dve", "pe", chain_mark[(g_, tc, "g")])
                            ve.wait_ge(clk.sem["dve"], bias_dve)
                            clk.mark(ve.scalar_tensor_tensor(gpre[:], pz[pzi][:], self.scg[:, tc:tc + 1],
                                                             gbb[:], ALU.mult, ALU.add), "dve", 1)
                        evac_mark[(g_, tc, "g")] = clk.cnt["dve"]

                        @self.on("act")
                        def _(se, g_=g_, tc=tc):
                            clk.wait(se, "act", "dve", evac_mark[(g_, tc, "g")])
                            clk.mark(se.activation(gsil[:], gpre[:], AF.Silu), "act", 1)
                        evac_mark[(g_, tc, "sil")] = clk.cnt["act"]
                    else:
                        @self.on("dve")
                        def _(ve, g_=g_, tc=tc, pzi=pzi):
                            clk.wait(ve, "dve", "pe", chain_mark[(g_, tc, "v")])
                            prevz = zdma_mark.get((g_, tc - 2)) or zdma_mark.get((g_ - 1, 6 + tc))
                            if prevz:
                                clk.wait(ve, "dve", "dact", prevz)
                            clk.mark(ve.scalar_tensor_tensor(vpre[:], pz[pzi][:], self.scv[:, tc:tc + 1],
                                                             vbb[:], ALU.mult, ALU.add), "dve", 1)
                            clk.wait(ve, "dve", "act", evac_mark[(g_, tc, "sil")])
                            ve.wait_ge(clk.sem["dve"], clk.cnt["dve"])
                            clk.mark(ve.tensor_mul(zsl[tc % 2][:], gsil[:], vpre[:]), "dve", 1)
                            ve.wait_ge(clk.sem["dve"], clk.cnt["dve"])
                            clk.mark(ve.tensor_reduce(zred[:], zsl[tc % 2][:], AX.X, ALU.max,
                                                      apply_absolute_value=True), "dve", 1)
                            ve.wait_ge(clk.sem["dve"], clk.cnt["dve"])
                            if g_ == 0:
                                clk.wait(ve, "dve", "pool", gz_init)
                            clk.mark(ve.tensor_max(gzcol[:, tc:tc + 1], gzcol[:, tc:tc + 1], zred[:]), "dve", 1)
                            ve.wait_ge(clk.sem["dve"], clk.cnt["dve"])
                            clk.mark(self.settle(ve), "dve", 1)
                        evac_mark[(g_, tc, "v")] = clk.cnt["dve"]

                        @self.on("act")
                        def _(se, g_=g_, tc=tc):
                            clk.wait(se, "act", "dve", evac_mark[(g_, tc, "v")])
                            clk.mark(se.dma_start(
                                out=self.zdr[128 * tc:128 * (tc + 1), 512 * g_:512 * (g_ + 1)],
                                in_=zsl[tc % 2][:]), "dact", 16)
                        zdma_mark[(g_, tc)] = clk.cnt["dact"]

        self.z_done = clk.cnt["dact"]
        self.free("pb2", "pz3", "pz2", "pz1", "pz0")
        self.free("zred", "zsl1", "zsl0", "vpre", "gsil", "gpre", "vbb", "gbb",
                  "vbrow", "gbrow", "wgv1", "wgv0", "wgg1", "wgg0")
        self.free("scv", "scg")
        self.free(*[f"qT{j}" for j in reversed(range(16))])
        self.free(*[f"OT{j}" for j in reversed(range(8))])

        dbg = self.dbg("z", [HT, H])
        if dbg is not None:
            @self.on("pool")
            def _(g):
                clk.wait(g, "pool", "dact", self.z_done)
                clk.mark(g.dma_start(out=dbg, in_=self.zdr), "dpool", 16)

    def p_qzT(self):
        if STAGE < 8:
            return
        clk = self.clk
        gzcol = self.bufs["gzcol"][1]
        gamcols = self.bufs["gamcols"][1]
        ident_f = self.bufs["ident_f"][1]
        ones1_f = self.bufs["ones1_f"][1]
        self.qzT = [self.sb(f"qzT{j}", [128, 1024], BF16) for j in range(64)]
        self.sob = self.sb("sob", [128, 1024], F32)
        sorow = self.sb("sorow", [1, 1024], F32)
        socol = self.sb("socol", [128, 8], F32)
        szcol = self.sb("szcol", [128, 8], F32)
        zl = [self.sb(f"zl{i}", [128, 4096], F32) for i in range(2)]
        qz8 = self.sb("qz8", [128, 4096], BF16)
        pso = self.ps("pso", [128, 1024], F32)

        @self.on("dve")
        def _(ve):
            ve.wait_ge(clk.sem["dve"], clk.cnt["dve"])
            clk.mark(ve.tensor_scalar(gzcol[:], gzcol[:], 1e-5, None, ALU.max), "dve", 1)
            ve.wait_ge(clk.sem["dve"], clk.cnt["dve"])
            ve.tensor_scalar(socol[:], gzcol[:], gamcols[:, 2:3], None, ALU.mult)
            clk.mark(ve.reciprocal(szcol[:], gzcol[:]), "dve", 1)
            ve.wait_ge(clk.sem["dve"], clk.cnt["dve"])
            clk.mark(ve.tensor_scalar(szcol[:], szcol[:], 127.0, None, ALU.mult), "dve", 1)
        sz_mark = clk.cnt["dve"]

        @self.on("pe")
        def _(te):
            clk.wait(te, "pe", "dve", sz_mark)
            last = None
            for i in range(8):
                last = te.matmul(pso[0:1, 128 * i:128 * (i + 1)], socol[:, i:i + 1], ident_f[:],
                                 start=True, stop=True)
            clk.mark(last, "pe", 1)

        @self.on("dve")
        def _(ve):
            clk.wait(ve, "dve", "pe", clk.cnt["pe"])
            clk.mark(ve.tensor_copy(sorow[:], pso[0:1, :]), "dve", 1)
        sorow_mark = clk.cnt["dve"]

        @self.on("pe")
        def _(te):
            clk.wait(te, "pe", "dve", sorow_mark)
            te.matmul(pso[:, 0:512], ones1_f[:], sorow[0:1, 0:512], start=True, stop=True)
            clk.mark(te.matmul(pso[:, 512:1024], ones1_f[:], sorow[0:1, 512:1024], start=True, stop=True), "pe", 1)

        @self.on("dve")
        def _(ve):
            clk.wait(ve, "dve", "pe", clk.cnt["pe"])
            clk.mark(ve.tensor_copy(self.sob[:], pso[:]), "dve", 1)
        self.sob_done = clk.cnt["dve"]

        dbgsob = self.dbg("sob", [128, 1024])
        if dbgsob is not None:
            @self.on("act")
            def _(se):
                clk.wait(se, "act", "dve", self.sob_done)
                clk.mark(se.dma_start(out=dbgsob, in_=self.sob[:]), "dact", 16)
        dbgz = self.dbg("gz", [128, 16])
        if dbgz is not None:
            @self.on("act")
            def _(se):
                clk.wait(se, "act", "dve", clk.cnt["dve"])
                clk.mark(se.dma_start(out=dbgz[:, 0:8], in_=gzcol[:]), "dact", 16)
                clk.mark(se.dma_start(out=dbgz[:, 8:16], in_=szcol[:]), "dact", 16)

        qz_mark = {}
        for i in range(8):
            for half in range(2):
                hb = (2 * i + half) % 2

                @self.on("sp")
                def _(sp, i=i, half=half, hb=hb):
                    clk.wait(sp, "sp", "dact", self.z_done)
                    prev = qz_mark.get((i, half - 2)) or qz_mark.get((i - 1, half))
                    if prev:
                        clk.wait(sp, "sp", "dve", prev)
                    clk.mark(sp.dma_start(out=zl[hb][:],
                                          in_=self.zdr[128 * i:128 * (i + 1),
                                                       4096 * half:4096 * (half + 1)]), "dsp", 16)
                dmam = clk.cnt["dsp"]

                @self.on("dve")
                def _(ve, i=i, half=half, hb=hb, dmam=dmam):
                    clk.wait(ve, "dve", "dsp", dmam)
                    ve.wait_ge(clk.sem["dve"], sz_mark)
                    clk.mark(ve.tensor_scalar(zl[hb][:], zl[hb][:], szcol[:, i:i + 1], MAGIC,
                                              ALU.mult, ALU.add), "dve", 1)
                    ve.wait_ge(clk.sem["dve"], clk.cnt["dve"])
                    clk.wait(ve, "dve", "pe", getattr(self, "qz8_free", 0))
                    clk.mark(ve.tensor_scalar(qz8[:], zl[hb][:], -MAGIC, None, ALU.add), "dve", 1)
                qz_mark[(i, half)] = clk.cnt["dve"]

                jobs = [(qz8[:, 128 * j:128 * (j + 1)],
                         self.qzT[32 * half + j][:, 128 * i:128 * (i + 1)]) for j in range(32)]
                self.transpose_batch(f"qz{i}_{half}", jobs, ("dve", qz_mark[(i, half)]))
                self.qz8_free = clk.cnt["pe"]
        self.qzT_done = clk.cnt["dve"]
        dbgq = self.dbg("qzT", [H, 1024], BF16)
        if dbgq is not None:
            @self.on("act")
            def _(se):
                clk.wait(se, "act", "dve", self.qzT_done)
                for j in range(64):
                    clk.mark(se.dma_start(out=dbgq[128 * j:128 * (j + 1), :], in_=self.qzT[j][:]), "dact", 16)
        self.free("pso")
        self.free("qz8", "zl1", "zl0", "szcol", "socol", "sorow")

    def p_outproj(self):
        if STAGE < 8:
            return
        clk = self.clk
        ident_f = self.bufs["ident_f"][1]
        bofo_col = self.bufs["bofo_col"][1]
        otbig = self.sb("otbig", [128, 64 * 256], BF16)
        atile = [self.sb(f"atile{i}", [128, 512], F32) for i in range(4)]
        xtile = [self.sb(f"xtile{i}", [128, 256], F32) for i in range(8)]
        resT = [self.sb(f"resT{i}", [128, 1024], F32) for i in range(2)]
        ystg = [self.sb(f"ystg{i}", [128, 256], F32) for i in range(2)]
        po = [self.ps(f"po{i}", [128, 512], F32) for i in range(4)]
        ptr2 = [self.ps(f"ptr2_{i}", [128, 512], F32) for i in range(2)]
        ago_t = self.ago_out.rearrange("(a p) f -> a p f", p=128)
        dgs = [(2 * i, 2) for i in range(8)]

        mm_mark = {}
        res_mark = {}
        y_mark = {}
        yadd = {}
        for gi, (d0, nd) in enumerate(dgs):
            @self.on("sp")
            def _(sp, gi=gi, d0=d0, nd=nd):
                clk.wait(sp, "sp", "cc", self.ag_mark["o"])
                if gi == 0:
                    clk.wait(sp, "sp", "dve", clk.cnt["dve"])
                else:
                    clk.wait(sp, "sp", "pe", mm_mark[(gi - 1, 63)])
                for hh in range(64):
                    clk.mark(sp.dma_start(out=otbig[:, 256 * hh:256 * hh + 128 * nd],
                                          in_=ago_t[hh][:, 128 * d0:128 * (d0 + nd)]), "dsp", 16)
            ot_dma = clk.cnt["dsp"]

            @self.on("sp")
            def _(sp, gi=gi, d0=d0, nd=nd):
                clk.wait(sp, "sp", "cc", self.rs_mark)
                if gi >= 1:
                    clk.wait(sp, "sp", "dve", res_mark[gi - 1])
                    clk.wait(sp, "sp", "dve", yadd[(gi - 1, 7)])
                for di in range(nd):
                    for tbk in range(2):
                        dc = d0 + di
                        clk.mark(sp.dma_start(out=atile[2 * di + tbk][:],
                                              in_=self.rs_out[128 * dc:128 * (dc + 1),
                                                              512 * tbk:512 * (tbk + 1)]), "dsp", 16)
                for t in range(8):
                    clk.mark(sp.dma_start(out=xtile[t][:, 0:128 * nd],
                                          in_=self.xhalf[128 * t:128 * (t + 1),
                                                         128 * d0:128 * (d0 + nd)]), "dsp", 16)
            at_dma = clk.cnt["dsp"]

            for hh in range(64):
                @self.on("pe")
                def _(te, gi=gi, d0=d0, nd=nd, hh=hh):
                    if hh == 0:
                        clk.wait(te, "pe", "dsp", ot_dma)
                        clk.wait(te, "pe", "dve", self.qzT_done)
                        if gi > 0:
                            clk.wait(te, "pe", "dve", res_mark[gi - 1])
                    last = None
                    for di in range(nd):
                        for tbk in range(2):
                            last = te.matmul(po[2 * di + tbk][:], otbig[:, 256 * hh + 128 * di:256 * hh + 128 * (di + 1)],
                                             self.qzT[hh][:, 512 * tbk:512 * (tbk + 1)],
                                             start=(hh == 0), stop=(hh == 63))
                    if hh == 63:
                        last = self.fence(te, po[2 * nd - 1][:], 512)
                    clk.mark(last, "pe", 1)
                mm_mark[(gi, hh)] = clk.cnt["pe"]

            for di in range(nd):
                for tbk in range(2):
                    @self.on("dve")
                    def _(ve, gi=gi, d0=d0, di=di, tbk=tbk):
                        clk.wait(ve, "dve", "pe", mm_mark[(gi, 63)])
                        clk.wait(ve, "dve", "dsp", at_dma)
                        ve.wait_ge(clk.sem["dve"], self.sob_done)
                        dc = d0 + di
                        sl = resT[di][:, 512 * tbk:512 * (tbk + 1)]
                        clk.mark(ve.tensor_mul(sl, po[2 * di + tbk][:], self.sob[:, 512 * tbk:512 * (tbk + 1)]),
                                 "dve", 1)
                        ve.wait_ge(clk.sem["dve"], clk.cnt["dve"])
                        clk.mark(ve.scalar_tensor_tensor(sl, sl, bofo_col[:, dc:dc + 1],
                                                         atile[2 * di + tbk][:],
                                                         ALU.add, ALU.add), "dve", 1)

            @self.on("dve")
            def _(ve):
                ve.wait_ge(clk.sem["dve"], clk.cnt["dve"])
                clk.mark(self.settle(ve), "dve", 1)
            res_mark[gi] = clk.cnt["dve"]

            dbgr = self.dbg("resT", [D, HT])
            if dbgr is not None:
                @self.on("act")
                def _(se, gi=gi, d0=d0, nd=nd):
                    clk.wait(se, "act", "dve", res_mark[gi])
                    for di in range(nd):
                        clk.mark(se.dma_start(out=dbgr[128 * (d0 + di):128 * (d0 + di + 1), :],
                                              in_=resT[di][:]), "dact", 16)

            for t in range(8):
                tb2 = t % 2

                @self.on("pe")
                def _(te, gi=gi, t=t, tb2=tb2, nd=nd):
                    clk.wait(te, "pe", "dve", res_mark[gi])
                    prev = yadd.get((gi, t - 2)) or yadd.get((gi - 1, 6 + t))
                    if prev:
                        clk.wait(te, "pe", "dve", prev)
                    last = None
                    for di in range(nd):
                        last = te.matmul(ptr2[tb2][:, 128 * di:128 * (di + 1)],
                                         resT[di][:, 128 * t:128 * (t + 1)], ident_f[:],
                                         start=True, stop=True)
                    clk.mark(self.fence(te, ptr2[tb2][:, 128 * (nd - 1):128 * nd], 128), "pe", 1)
                trm = clk.cnt["pe"]

                @self.on("dve")
                def _(ve, gi=gi, t=t, tb2=tb2, nd=nd, trm=trm):
                    clk.wait(ve, "dve", "pe", trm)
                    clk.wait(ve, "dve", "dsp", at_dma)
                    prev = y_mark.get((gi, t - 2)) or y_mark.get((gi - 1, 6 + t))
                    if prev:
                        clk.wait(ve, "dve", "dact", prev)
                    clk.mark(ve.tensor_add(ystg[tb2][:, 0:128 * nd], ptr2[tb2][:, 0:128 * nd],
                                           xtile[t][:, 0:128 * nd]), "dve", 1)
                    ve.wait_ge(clk.sem["dve"], clk.cnt["dve"])
                    clk.mark(self.settle(ve), "dve", 1)
                yadd[(gi, t)] = clk.cnt["dve"]

                @self.on("act")
                def _(se, gi=gi, d0=d0, nd=nd, t=t, tb2=tb2):
                    clk.wait(se, "act", "dve", yadd[(gi, t)])
                    clk.mark(se.dma_start(out=self.y[128 * t:128 * (t + 1), 128 * d0:128 * (d0 + nd)],
                                          in_=ystg[tb2][:, 0:128 * nd]), "dact", 16)
                y_mark[(gi, t)] = clk.cnt["dact"]

        self.free("ptr2_1", "ptr2_0", "po3", "po2", "po1", "po0")
        self.free("ystg1", "ystg0", "resT1", "resT0", "xtile7", "xtile6", "xtile5", "xtile4", "xtile3", "xtile2", "xtile1", "xtile0",
                  "atile3", "atile2", "atile1", "atile0", "otbig")
        self.free("sob", *[f"qzT{j}" for j in reversed(range(64))])

    def p_finish(self):
        clk = self.clk

        if STAGE < 8:
            yst = self.sb("yst", [128, D], F32)

            @self.on("pool")
            def _(g):
                clk.wait(g, "pool", "dve", clk.cnt["dve"])
                clk.wait(g, "pool", "act", clk.cnt["act"])
                clk.wait(g, "pool", "pe", clk.cnt["pe"])
                clk.mark(g.memset(yst[:], 0.0), "pool", 1)

            @self.on("act")
            def _(se):
                clk.wait(se, "act", "pool", clk.cnt["pool"])
                ytiles = self.y.rearrange("(a p) f -> a p f", p=128)
                for i in range(8):
                    clk.mark(se.dma_start(out=ytiles[i], in_=yst[:]), "dact", 16)

        @self.on("act")
        def _(se):
            clk.wait(se, "act", "dact", clk.cnt["dact"])
            clk.wait(se, "act", "dsp", clk.cnt["dsp"])
            clk.wait(se, "act", "dpool", clk.cnt["dpool"])
            clk.wait(se, "act", "cc", clk.cnt["cc"])


def shard_inputs(ins):
    x = np.ascontiguousarray(ins["L0_attn_x"])
    Wqkv = np.ascontiguousarray(ins["L0_attn_W_qkv"])
    bqkv = np.ascontiguousarray(ins["L0_attn_b_qkv"]).reshape(1, -1)
    Wout = np.ascontiguousarray(ins["L0_attn_W_out"])
    bout = np.ascontiguousarray(ins["L0_attn_b_out"]).reshape(1, -1)
    gw = np.ascontiguousarray(ins["L0_ffn_gate_w"])
    gb = np.ascontiguousarray(ins["L0_ffn_gate_b"]).reshape(1, -1)
    vw = np.ascontiguousarray(ins["L0_ffn_val_w"])
    vb = np.ascontiguousarray(ins["L0_ffn_val_b"]).reshape(1, -1)
    ow = np.ascontiguousarray(ins["L0_ffn_out_w"])
    ob = np.ascontiguousarray(ins["L0_ffn_out_b"]).reshape(1, -1)

    maps = []
    for c in range(8):
        b, p = c // 2, c % 2
        hs = slice(1024 * p, 1024 * (p + 1))
        m = {
            "xseq": x[b],
            "xhalf": np.ascontiguousarray(x[b, 1024 * p:1024 * (p + 1)]),
            "wqkv": np.ascontiguousarray(np.concatenate(
                [Wqkv[:, base + 128 * (8 * p + h):base + 128 * (8 * p + h + 1)]
                 for h in range(8) for base in (0, 2048, 4096)], axis=1)),
            "bqkv": np.ascontiguousarray(np.concatenate(
                [bqkv[:, base + 128 * (8 * p + h):base + 128 * (8 * p + h + 1)]
                 for h in range(8) for base in (0, 2048, 4096)], axis=1)),
            "wout": np.ascontiguousarray(Wout[hs, :]),
            "bout": bout,
            "fob": ob,
            "gbi": gb,
            "vbi": vb,
            "gsh": np.ascontiguousarray(gw[:, 256 * c:256 * (c + 1)]),
            "vsh": np.ascontiguousarray(vw[:, 256 * c:256 * (c + 1)]),
            "osh": np.ascontiguousarray(ow[:, 1024 * c:1024 * (c + 1)]),
        }
        maps.append(m)
    return maps


_CACHED = {}


def get_nc():
    key = tuple(sorted(DBG))
    if key not in _CACHED:
        _CACHED[key] = KB()
        _CACHED[key].nc_built = _CACHED[key].build()
    return _CACHED[key]


def kernel(**inputs):
    kb = get_nc()
    maps = shard_inputs(inputs)
    res = run_bass_kernel_spmd(kb.nc_built, maps, core_ids=list(range(8)))
    out = np.empty((4, 2048, 2048), np.float32)
    for c in range(8):
        b, p = c // 2, c % 2
        out[b, 1024 * p:1024 * (p + 1)] = res.results[c]["y"]
    kernel.last_results = res.results
    return out
